# version 20
# baseline (speedup 1.0000x reference)
import numpy as np
import ml_dtypes
import concourse.bass as bass
import concourse.mybir as mybir
from concourse.bass_utils import run_bass_kernel_spmd
from concourse.hw_specs import get_activation_tables

F32 = mybir.dt.float32
BF16 = mybir.dt.bfloat16
AX = mybir.AxisListType
ALU = mybir.AluOpType
ACT = mybir.ActivationFunctionType

DM = 1024
H = 16
DH = 64
NP = 8
EC = 8
FF = 4096
NFC = 32
CHUNK = 64
SLAB = 512
CSLAB = 256
BASE_LR = 0.001
BASE_WD = 0.9
EPS = 1e-5
NG = SLAB // 128
NCHS = SLAB // CHUNK


class SemC:
    def __init__(self, handle, step=1):
        self.h = handle
        self.step = step
        self.n = 0


class Emit:
    ENGS = ("sync", "gpsimd", "scalar", "vector", "tensor")

    def __init__(self, nc):
        self.nc = nc
        self.q = {e: [] for e in self.ENGS}
        self.frozen = False

    def wait(self, eng, sem, val):
        if self.frozen:
            return
        if val > 0:
            self.q[eng].append(lambda E, s=sem.h, v=val: E.wait_ge(s, v))

    def op(self, eng, fn, inc=None):
        if self.frozen:
            return 0
        if inc is None:
            self.q[eng].append(fn)
            return None
        inc.n += inc.step
        v = inc.n
        self.q[eng].append(lambda E, f=fn, s=inc: f(E).then_inc(s.h, s.step))
        return v

    def replay(self, block):
        for eng in self.ENGS:
            ops = self.q[eng]

            def make(ops):
                def run(E):
                    for f in ops:
                        f(E)
                return run
            getattr(block, eng)(make(ops))


def _act_set_idx(nc, want):
    tabs = get_activation_tables(nc.m.arch)
    for i, (name, funcs) in enumerate(tabs.items()):
        if all(f in funcs for f in want):
            return i
    raise ValueError(f"no table set with {want}")


def _table_load(E_, idx, nc):
    return E_.add_instruction(mybir.InstLoadActFuncSet(
        name=nc.get_next_instruction_name(), act_func_set_id=idx, ins=[], outs=[]))


def build_nc(L=4096, trunc=None, dbg=False, dump=()):
    NSLAB = L // SLAB
    NCS = L // CSLAB
    NCHT = L // CHUNK
    nc = bass.Bass("TRN2", target_bir_lowering=False, debug=False,
                   detect_race_conditions=False)

    set_exp = _act_set_idx(nc, {ACT.Exp, ACT.Ln, ACT.Copy})
    set_silu = _act_set_idx(nc, {ACT.Silu, ACT.Tanh, ACT.Copy})
    set_gelu = _act_set_idx(nc, {ACT.Gelu, ACT.Copy})

    dt_ = nc.dram_tensor
    x_d = dt_("x", [NSLAB, 128, EC, SLAB], F32, kind="ExternalInput").ap()
    y_d = dt_("y", [NCS, 128, EC, CSLAB], F32, kind="ExternalOutput").ap()
    wq_d = dt_("wq", [EC, 128, DM], F32, kind="ExternalInput").ap()
    wk_d = dt_("wk", [EC, 128, DM], F32, kind="ExternalInput").ap()
    wv_d = dt_("wv", [EC, 128, DM], F32, kind="ExternalInput").ap()
    wlb_d = dt_("wlb", [EC, 128, 64], F32, kind="ExternalInput").ap()
    wo_d = dt_("wo", [EC, 128, DM], F32, kind="ExternalInput").ap()
    w1_d = dt_("w1", [EC, 128, FF], F32, kind="ExternalInput").ap()
    w2_d = dt_("w2", [NFC, 128, DM], F32, kind="ExternalInput").ap()
    b1_d = dt_("b1a", [128, NFC], F32, kind="ExternalInput").ap()
    b2_d = dt_("b2a", [128, EC], F32, kind="ExternalInput").ap()
    wini_d = dt_("wini", [128, NP * DH], F32, kind="ExternalInput").ap()
    wo1i_d = dt_("wo1i", [128, NP * DH], F32, kind="ExternalInput").ap()
    wo2i_d = dt_("wo2i", [128, NP * DH], F32, kind="ExternalInput").ap()
    idn_d = dt_("idn", [128, 128], F32, kind="ExternalInput").ap()
    selm_d = dt_("selm", [64, 4 * 128], F32, kind="ExternalInput").ap()
    maskp_d = dt_("maskp", [64, NP], F32, kind="ExternalInput").ap()
    ones1_d = dt_("ones1", [1, 128], F32, kind="ExternalInput").ap()
    onesc_d = dt_("onesc", [128, 1], F32, kind="ExternalInput").ap()
    cst_d = dt_("cst", [128, 8], F32, kind="ExternalInput").ap()
    pick_d = dt_("pick", [8, 8 * 128], F32, kind="ExternalInput").ap()
    _scr = "ExternalOutput" if dbg else "Internal"
    x2s_d = dt_("x2s", [NSLAB, 128, EC, SLAB], F32, kind=_scr).ap()
    w1ts_d = dt_("w1ts", [EC, 128, FF], BF16, kind=_scr).ap()
    w2ts_d = dt_("w2ts", [NFC, 128, DM], BF16, kind=_scr).ap()
    qks_d = dt_("qks", [NSLAB, 128, NP, 2, SLAB], BF16, kind=_scr).ap()
    vs_d = dt_("vs", [NSLAB, 128, NP, SLAB], BF16, kind=_scr).ap()

    SB_BASE = (nc.sbuf_base + 31) & ~31
    SB_LIMIT = nc.sbuf_top

    ARENA_TOP = SB_LIMIT - 4096
    arena = nc.alloc_sbuf_tensor_at(
        "arena", [128, ARENA_TOP - SB_BASE], mybir.dt.uint8, offset=SB_BASE)

    class SAlloc:

        def __init__(self, base):
            self.off = base

        def t(self, name, shape, dtype):
            per = int(np.prod(shape[1:])) * mybir.dt.size(dtype)
            self.off = (self.off + 31) & ~31
            a0 = self.off - SB_BASE
            _LAYOUT[name] = (self.off, per)
            ap = arena[0:shape[0], a0:a0 + per].bitcast(dtype)
            if len(shape) > 2:
                names = " ".join(f"d{i}" for i in range(1, len(shape)))
                kw = {f"d{i}": shape[i] for i in range(2, len(shape))}
                ap = ap.rearrange(f"p ({names}) -> p {names}", **kw)
            self.off += per
            return ap

    P = SAlloc(SB_BASE)
    ident_b = P.t("ident_b", [128, 128], BF16)
    selm_sb = P.t("selm_sb", [64, 4, 128], F32)
    maskp_sb = P.t("maskp_sb", [64, NP], F32)
    ones1_sb = P.t("ones1_sb", [1, 128], F32)
    onesc_sb = P.t("onesc_sb", [128, 1], F32)
    onesb_sb = P.t("onesb_sb", [128, 1], BF16)
    b1_sb = P.t("b1_sb", [128, NFC], F32)
    b2_sb = P.t("b2_sb", [128, EC], F32)
    stats1_sb = P.t("stats1_sb", [128, NSLAB, 8], F32)
    stats2_sb = P.t("stats2_sb", [128, NSLAB, 8], F32)
    lrwdc_all = P.t("lrwdc_all", [64, NSLAB, NCHS], F32)
    thr_col = P.t("thr_col", [128, 4], F32)
    st8b = P.t("st8b", [128, 8], BF16)
    pick_sb = P.t("pick_sb", [8, 8, 128], BF16)
    st_sb = P.t("st_sb", [128, 16], F32)
    rows_sb = P.t("rows_sb", [8, 128], BF16)
    wtpart = P.t("wtpart", [128, EC + NFC], F32)
    wtot = P.t("wtot", [128, 2], F32)
    trow = P.t("trow", [1, 4], F32)
    cst_sb = P.t("cst_sb", [128, 8], F32)
    nc.const_aps.register(F32, EPS, cst_sb[:, 0:1])
    nc.const_aps.register(F32, -0.5, cst_sb[:, 1:2])
    nc.const_aps.register(F32, 0.5, cst_sb[:, 2:3])
    nc.const_aps.register(F32, 0.0, cst_sb[:, 3:4])
    nc.const_aps.register(F32, 1.0, cst_sb[:, 4:5])
    nc.const_aps.register(BF16, 0.0, cst_sb[:, 5:6].bitcast(BF16)[:, 0:1])
    nc.const_aps.register(BF16, 1.0, cst_sb[:, 5:6].bitcast(BF16)[:, 1:2])
    PERS_END = P.off

    Sh = SAlloc(PERS_END)
    x_sl = [Sh.t(f"x_sl{i}", [128, EC, SLAB], F32) for i in range(2)]
    win_f = [Sh.t(f"win_f{i}", [128, NP, DH], F32) for i in range(2)]
    wo1_f = [Sh.t(f"wo1_f{i}", [128, NP, DH], F32) for i in range(2)]
    wo2_f = [Sh.t(f"wo2_f{i}", [128, NP, DH], F32) for i in range(2)]
    win_b = Sh.t("win_b", [128, NP, DH], BF16)
    wo1_b = Sh.t("wo1_b", [128, NP, DH], BF16)
    wo2_b = Sh.t("wo2_b", [128, NP, DH], BF16)
    wdec_s = Sh.t("wdec_s", [128, NP, DH], F32)
    gs_s = Sh.t("gs_s", [128, NP, DH], F32)
    xsq_b = [Sh.t(f"xsq_b{i}", [128, SLAB], BF16) for i in range(2)]
    xb2_b = [Sh.t(f"xb2_b{i}", [128, SLAB], BF16) for i in range(2)]
    SH_END = Sh.off

    A = SAlloc(SH_END)
    wo_sb = A.t("wo_sb", [128, EC, DM], BF16)
    qk_sl = [A.t(f"qk_sl{i}", [128, NP, 2, SLAB], BF16) for i in range(2)]
    v_sl = [A.t(f"v_sl{i}", [128, NP, SLAB], BF16) for i in range(2)]
    kt_sl = A.t("kt_sl", [128, SLAB // 128, NP, 128], BF16)
    vt_sl = A.t("vt_sl", [128, SLAB // 128, NP, 128], BF16)
    o_sl = [A.t(f"o_sl{i}", [128, NP, SLAB], BF16) for i in range(2)]
    x2_sl = A.t("x2_sl", [128, EC, SLAB], F32)
    e_sb = A.t("e_sb", [128, 2, 512], BF16)
    p_sb = A.t("p_sb", [128, 2, 512], BF16)
    sums_sb = A.t("sums_sb", [128, 16], F32)
    rcp_sb = A.t("rcp_sb", [128, 16], F32)
    pt_sb = A.t("pt_sb", [128, 4, 128], BF16)
    ktr = [A.t(f"ktr{i}", [128, NP, 128], BF16) for i in range(2)]
    vtr = [A.t(f"vtr{i}", [128, NP, 128], BF16) for i in range(2)]
    pa_sb = A.t("pa_sb", [128, 4, 2, DH], F32)
    s_sb = A.t("s_sb", [128, 8], F32)
    u_sb = A.t("u_sb", [128, 4, 2, DH], BF16)
    gz_sb = A.t("gz_sb", [128, 4, 128], BF16)
    maps_sb = A.t("maps_sb", [128, 4, NP, NCHS], F32)
    scat_sb = A.t("scat_sb", [64, NP, NCHS], F32)
    wdec1_s = A.t("wdec1_s", [128, NP, DH], F32)
    wdec2_s = A.t("wdec2_s", [128, NP, DH], F32)
    gs_s1 = A.t("gs_s1", [128, NP, DH], F32)
    gs_s2 = A.t("gs_s2", [128, NP, DH], F32)
    aTs_sb = A.t("aTs_sb", [128, 2, 256], F32)
    A_END = A.off
    assert A_END <= ARENA_TOP, f"P2 SBUF over: {A_END} > {ARENA_TOP}"

    def off_of(h):
        return nc.lookup_mloc(h).addr

    B = SAlloc(SH_END)
    wq_sb = B.t("wq_sb", [128, EC, DM], BF16)
    wk_sb = B.t("wk_sb", [128, EC, DM], BF16)
    wv_sb = B.t("wv_sb", [128, EC, DM], BF16)
    wlb_sb = B.t("wlb_sb", [128, EC, 64], BF16)
    xln_sl2 = [B.t(f"xln_sl{i}", [128, EC, SLAB], BF16) for i in range(2)]
    qst_b = [B.t(f"qst_b{i}", [128, 512], BF16) for i in range(8)]
    wstage = [B.t(f"wstage{i}", [128, FF], F32) for i in range(2)]
    wtern = B.t("wtern", [128, FF], BF16)
    wtern2 = B.t("wtern2", [128, FF], BF16)
    t2buf = B.t("t2buf", [128, FF], BF16)
    rb_sb = B.t("rb_sb", [128, SLAB], F32)
    mrb_sb = B.t("mrb_sb", [128, SLAB], F32)
    tmp_sl = B.t("tmp_sl", [128, SLAB], F32)
    B_END = B.off
    assert B_END <= ARENA_TOP, f"P1 SBUF over: {B_END} > {ARENA_TOP}"

    C = SAlloc(PERS_END)
    w1t_sb = C.t("w1t_sb", [128, EC, FF], BF16)
    w2t_sb = C.t("w2t_sb", [128, NFC, DM], BF16)
    x2c = [C.t(f"x2c{i}", [128, EC, CSLAB], F32) for i in range(2)]
    x2ln_c = [C.t(f"x2ln_c{i}", [128, EC, CSLAB], BF16) for i in range(2)]
    outc = [C.t(f"outc{i}", [128, EC, CSLAB], F32) for i in range(2)]
    h_sb = [C.t(f"h_sb{i}", [128, CSLAB], BF16) for i in range(2)]
    rows2_sb = C.t("rows2_sb", [8, 128], BF16)
    repr_c = [C.t(f"repr_c{i}", [128, CSLAB], F32) for i in range(2)]
    repmr_c = [C.t(f"repmr_c{i}", [128, CSLAB], F32) for i in range(2)]
    tmp_c = C.t("tmp_c", [128, CSLAB], F32)
    C_END = C.off
    assert C_END <= ARENA_TOP, f"P3 SBUF over: {C_END} > {ARENA_TOP}"

    nc.alloc_sbuf_tensor("resv", [128, ARENA_TOP - nc.sbuf_base], mybir.dt.uint8)

    _pb = {}
    for i in (0, 1, 2):
        _pb[i] = nc.alloc_psum_tensor(f"psB{i}", [128, 512], F32)
    b34 = nc.alloc_psum_tensor("psB34", [128, 1024], F32)
    for i in (5, 6, 7):
        _pb[i] = nc.alloc_psum_tensor(f"psB{i}", [128, 512], F32)

    def PB(i):
        if i == 3:
            return b34[:, 0:512]
        if i == 4:
            return b34[:, 512:1024]
        return _pb[i][:, :]

    def sem(name, step=1):
        h = nc.ctx.enter_context(nc.semaphore(name))
        return SemC(h, step)

    dW = sem("dW", 16)
    dX = sem("dX", 16)
    dXp = [sem("dXp0", 16), sem("dXp1", 16)]
    dQp = [sem("dQp0", 16), sem("dQp1", 16)]
    dTlp = [sem("dTlp0", 16), sem("dTlp1", 16)]
    dCx = [sem("dCx0", 16), sem("dCx1", 16)]
    dQ = sem("dQ", 16)
    dR = sem("dR", 16)
    dX2 = sem("dX2", 16)
    dTl = sem("dTl", 16)
    dTw = sem("dTw", 16)
    dTw2 = sem("dTw2", 16)
    dC = sem("dC", 16)
    dY = sem("dY", 16)
    tT = sem("tT")
    tV = sem("tV")
    tS = sem("tS")
    tG = sem("tG")

    E = Emit(nc)

    def _ckpt(tag):
        if trunc is not None and trunc == tag:
            E.frozen = True

    S = {
        "cur": 0, "initcast": 0, "cast1": 0, "cast2": 0, "kvt": {}, "maps_v": {},
        "b5_free": 0, "b6_free": 0, "b7V": 0, "b7S": 0,
        "tern_g": 0, "tern_v": 0, "kvt_ev": 0,
        "exp_k": {}, "exp_q": {}, "oevac": {}, "ptevac": {}, "aTev": {},
        "pk": {}, "pq": {}, "pa_v": {}, "u": {}, "gz": {},
        "gs1": {}, "gs2": {}, "gs3": {}, "mm1k": {}, "mm1q": {}, "aT": {}, "o": {},
        "pT": {}, "G1": {}, "G2": {}, "Gin": {}, "rep": {}, "xev": {},
        "repz": 0, "pw": {}, "u1f": {}, "u2f": {}, "u3f": {}, "u3b": {},
        "womm": 0,
    }

    def gdma(out, in_, s=dW):
        return E.op("gpsimd", lambda E_, o=out, i=in_: E_.dma_start(out=o, in_=i), inc=s)

    gdma(win_f[0][:, :, :], wini_d[:, :].rearrange("p (a b) -> p a b", b=DH))
    gdma(wo1_f[0][:, :, :], wo1i_d[:, :].rearrange("p (a b) -> p a b", b=DH))
    gdma(wo2_f[0][:, :, :], wo2i_d[:, :].rearrange("p (a b) -> p a b", b=DH))
    gdma(ident_b[:, :], idn_d[:, :])
    gdma(selm_sb[:, :, :], selm_d[:, :].rearrange("p (a b) -> p a b", b=128))
    gdma(maskp_sb[:, :], maskp_d[:, :])
    gdma(ones1_sb[:, :], ones1_d[:, :])
    gdma(onesc_sb[:, :], onesc_d[:, :])
    gdma(onesb_sb[:, :], onesc_d[:, :])
    gdma(b1_sb[:, :], b1_d[:, :])
    gdma(b2_sb[:, :], b2_d[:, :])
    gdma(cst_sb[:, :], cst_d[:, :])
    gdma(pick_sb[:, :, :], pick_d[:, :].rearrange("p (a b) -> p a b", b=128))
    W_CONST = dW.n

    gdma(wq_sb[:, :, :], wq_d[:, :, :].rearrange("e p m -> p e m"))
    gdma(wk_sb[:, :, :], wk_d[:, :, :].rearrange("e p m -> p e m"))
    gdma(wv_sb[:, :, :], wv_d[:, :, :].rearrange("e p m -> p e m"))
    gdma(wlb_sb[:, :, :], wlb_d[:, :, :].rearrange("e p m -> p e m"))
    W_LOADS = dW.n

    E.op("sync", lambda E_: E_.dma_start(out=x_sl[0][:, :, :], in_=x_d[0, :, :, :]), inc=dXp[0])
    XP1 = {0: 16}

    E.op("scalar", lambda E_: _table_load(E_, set_exp, nc))
    E.wait("scalar", dW, W_LOADS)
    E.op("scalar", lambda E_: E_.copy(out=win_b[:, :, :], in_=win_f[0][:, :, :]))
    E.op("scalar", lambda E_: E_.copy(out=wo1_b[:, :, :], in_=wo1_f[0][:, :, :]))
    S["initcast"] = S["cast1"] = S["cast2"] = E.op("scalar", lambda E_: E_.copy(
        out=wo2_b[:, :, :], in_=wo2_f[0][:, :, :]), inc=tS)

    E.op("vector", lambda E_: E_.memset(b34[:, :], 0.0))
    if dump:
        for _b in (o_sl[0], o_sl[1], p_sb, pt_sb, gz_sb, e_sb, s_sb, pa_sb,
                   u_sb, sums_sb, rcp_sb, x2_sl, maps_sb):
            E.op("vector", lambda E_, b=_b: E_.memset(
                b if len(b.shape) == 2 else b, 0.0))

    def emit_ln_stats(xb_of, cast_of, store_t, store_s):
        sum_mm = {}
        sq_mm = {}
        sq_v = {}
        E.wait("tensor", tV, S["b5_free"])
        E.wait("tensor", tV, S["b6_free"])
        for ec in range(EC):
            cv = cast_of(ec, sum_mm.get(ec - 2, 0), sq_v.get(ec - 2, 0))
            E.wait("tensor", tS, cv)
            E.wait("vector", tS, cv)
            xb = xb_of(ec)
            for g in range(NG):
                last = E.op("tensor", lambda E_, xb=xb, g=g, ec=ec: E_.matmul(
                    PB(6)[:, ec * NG + g:ec * NG + g + 1],
                    xb[:, 128 * g:128 * (g + 1)], onesb_sb[:, :],
                    start=True, stop=True),
                    inc=(tT if g == NG - 1 else None))
            sum_mm[ec] = last
            if ec >= 2:
                E.wait("vector", tT, sq_mm[ec - 2])
            sq_v[ec] = E.op("vector", lambda E_, xb=xb, b=xsq_b[ec % 2]: E_.tensor_tensor(
                out=b[:, :], in0=xb, in1=xb, op=ALU.mult), inc=tV)
            E.wait("tensor", tV, sq_v[ec])
            for g in range(NG):
                last = E.op("tensor", lambda E_, b=xsq_b[ec % 2], g=g, ec=ec: E_.matmul(
                    PB(5)[:, ec * NG + g:ec * NG + g + 1],
                    b[:, 128 * g:128 * (g + 1)], onesb_sb[:, :],
                    start=True, stop=True),
                    inc=(tT if g == NG - 1 else None))
            sq_mm[ec] = last
        E.wait("vector", tT, sq_mm[EC - 1])
        E.op("vector", lambda E_: E_.reduce_sum(
            out=st_sb[:, 0:NG],
            in_=PB(6)[:, 0:NG * EC].rearrange("p (e g) -> p g e", g=NG),
            axis=AX.X))
        vq = E.op("vector", lambda E_: E_.reduce_sum(
            out=st_sb[:, 4:4 + NG],
            in_=PB(5)[:, 0:NG * EC].rearrange("p (e g) -> p g e", g=NG),
            axis=AX.X), inc=tV)
        E.wait("vector", tV, vq)
        S["b5_free"] = vq
        S["b6_free"] = vq
        vv = E.op("vector", lambda E_: E_.tensor_scalar(
            out=st_sb[:, 8:8 + NG], in0=st_sb[:, 0:NG], scalar1=1.0 / DM,
            scalar2=None, op0=ALU.mult), inc=tV)
        E.wait("vector", tV, vv)
        vv = E.op("vector", lambda E_: E_.tensor_tensor(
            out=st_sb[:, 12:12 + NG], in0=st_sb[:, 8:8 + NG],
            in1=st_sb[:, 8:8 + NG], op=ALU.mult), inc=tV)
        E.wait("vector", tV, vv)
        vvar = E.op("vector", lambda E_: E_.scalar_tensor_tensor(
            out=st_sb[:, 0:NG], in0=st_sb[:, 4:4 + NG], scalar=1.0 / DM,
            in1=st_sb[:, 12:12 + NG], op0=ALU.mult, op1=ALU.subtract), inc=tV)
        E.wait("scalar", tV, vvar)
        sv = E.op("scalar", lambda E_: E_.activation(
            out=st_sb[:, 12:12 + NG], in_=st_sb[:, 0:NG], func=ACT.Ln, bias=EPS), inc=tS)
        E.wait("scalar", tS, sv)
        srt = E.op("scalar", lambda E_: E_.activation(
            out=st_sb[:, 4:4 + NG], in_=st_sb[:, 12:12 + NG], func=ACT.Exp, scale=-0.5), inc=tS)
        E.wait("vector", tS, srt)
        vmr = E.op("vector", lambda E_: E_.tensor_tensor(
            out=st_sb[:, 0:NG], in0=st_sb[:, 8:8 + NG], in1=st_sb[:, 4:4 + NG],
            op=ALU.mult), inc=tV)
        E.wait("vector", tV, vmr)
        vst = E.op("vector", lambda E_, t=store_t, s=store_s: E_.tensor_copy(
            out=t[:, s, :], in_=st_sb[:, 0:8]), inc=tV)
        E.wait("vector", tV, vst)
        return vst

    def emit_bcast(stats_t, s, wait_T_extra=0):
        vc8 = E.op("vector", lambda E_, t=stats_t, s=s: E_.tensor_copy(
            out=st8b[:, :], in_=t[:, s, :]), inc=tV)
        E.wait("tensor", tV, vc8)
        E.wait("tensor", tV, S["b6_free"])
        if wait_T_extra:
            E.wait("tensor", tT, wait_T_extra)
        ttr = E.op("tensor", lambda E_: E_.transpose(
            PB(6)[:, :].bitcast(BF16)[0:8, 0:128], st8b[:, :], ident_b[:, :]), inc=tT)
        E.wait("vector", tT, ttr)
        vrw = E.op("vector", lambda E_: E_.tensor_copy(
            out=rows_sb[:, :], in_=PB(6)[:, :].bitcast(BF16)[0:8, 0:128]), inc=tV)
        S["b6_free"] = vrw
        E.wait("tensor", tV, vrw)
        E.wait("tensor", tV, S["b7V"])
        E.wait("tensor", tS, S["b7S"])
        for g in range(NG):
            last = E.op("tensor", lambda E_, g=g: E_.matmul(
                PB(7)[:, 128 * g:128 * (g + 1)], pick_sb[:, g, :], rows_sb[:, :],
                start=True, stop=True), inc=(tT if g == NG - 1 else None))
        E.wait("vector", tT, last)
        vm = E.op("vector", lambda E_: E_.tensor_copy(out=mrb_sb[:, :], in_=PB(7)[:, :]), inc=tV)
        E.wait("tensor", tV, vm)
        for g in range(NG):
            last = E.op("tensor", lambda E_, g=g: E_.matmul(
                PB(7)[:, 128 * g:128 * (g + 1)], pick_sb[:, 4 + g, :], rows_sb[:, :],
                start=True, stop=True), inc=(tT if g == NG - 1 else None))
        E.wait("vector", tT, last)
        vrb = E.op("vector", lambda E_: E_.tensor_copy(out=rb_sb[:, :], in_=PB(7)[:, :]), inc=tV)
        S["b7V"] = vrb
        return vrb

    TIDX = EC + NFC

    def tern_src(idx):
        if idx < EC:
            return w1_d[idx, :, :], w1ts_d[idx, :, :], FF, 0
        return w2_d[idx - EC, :, :], w2ts_d[idx - EC, :, :], DM, 2

    tern_state = {"last_g": {}, "thr_v": 0, "wr": {0: 0, 1: 0}}
    TORDER = []
    _w2i = iter(range(EC, TIDX))
    for _k in range(EC):
        TORDER.append(_k)
        for _ in range(4):
            _n2 = next(_w2i, None)
            if _n2 is not None:
                TORDER.append(_n2)

    def tern_scale_unit(pos):
        idx = TORDER[pos]
        src, _, wid, _ = tern_src(idx)
        buf = wstage[pos % 2]
        dl = dTlp[pos % 2]
        eng = "gpsimd" if pos % 2 == 0 else "sync"
        if pos >= 2:
            E.wait(eng, tG, tern_state["last_g"][pos - 2])
        E.op(eng, lambda E_, src=src, w=wid, b=buf: E_.dma_start(
            out=b[:, 0:w], in_=src), inc=dl)
        E.wait("vector", dl, dl.n)
        tern_state["last_g"][pos] = E.op("vector", lambda E_, w=wid, b=buf, idx=idx: E_.tensor_reduce(
            out=wtpart[:, idx:idx + 1], in_=b[:, 0:w], axis=AX.X,
            op=ALU.add, apply_absolute_value=True), inc=tG)

    def tern_thresholds():
        E.wait("vector", tG, tern_state["last_g"][TIDX - 1])
        va = E.op("vector", lambda E_: E_.reduce_sum(
            out=wtot[:, 0:1], in_=wtpart[:, 0:EC], axis=AX.X), inc=tV)
        E.wait("vector", tV, va)
        vb = E.op("vector", lambda E_: E_.reduce_sum(
            out=wtot[:, 1:2], in_=wtpart[:, EC:EC + NFC], axis=AX.X), inc=tV)
        E.wait("vector", tV, vb)
        E.wait("tensor", tV, vb)
        E.wait("tensor", tV, S["b6_free"])
        tt = E.op("tensor", lambda E_: E_.matmul(
            PB(6)[0:2, 0:1], wtot[:, 0:2], onesc_sb[:, :], start=True, stop=True), inc=tT)
        E.wait("vector", tT, tt)
        vv = E.op("vector", lambda E_: E_.tensor_scalar(
            out=PB(6)[0:2, 1:2], in0=PB(6)[0:2, 0:1], scalar1=1.0 / (FF * DM),
            scalar2=1e-8, op0=ALU.mult, op1=ALU.max), inc=tV)
        E.wait("vector", tV, vv)
        vv = E.op("vector", lambda E_: E_.tensor_scalar(
            out=PB(6)[0:2, 2:3], in0=PB(6)[0:2, 1:2], scalar1=0.5,
            scalar2=None, op0=ALU.mult), inc=tV)
        E.wait("vector", tV, vv)
        vv = E.op("vector", lambda E_: E_.tensor_scalar(
            out=PB(6)[0:2, 3:4], in0=PB(6)[0:2, 2:3], scalar1=-1.0,
            scalar2=None, op0=ALU.mult), inc=tV)
        E.wait("vector", tV, vv)
        v4 = E.op("vector", lambda E_: E_.tensor_copy(
            out=wtot[0:2, 0:2], in_=PB(6)[0:2, 2:4]), inc=tV)
        S["b6_free"] = v4
        E.wait("vector", tV, v4)
        E.wait("gpsimd", tV, v4)
        E.op("gpsimd", lambda E_: E_.dma_start(out=trow[0:1, 0:2], in_=wtot[0:1, 0:2]), inc=dTw)
        E.op("gpsimd", lambda E_: E_.dma_start(out=trow[0:1, 2:4], in_=wtot[1:2, 0:2]), inc=dTw)
        E.wait("tensor", dTw, dTw.n)
        E.wait("tensor", tV, S["b7V"])
        E.wait("tensor", tS, S["b7S"])
        tb = E.op("tensor", lambda E_: E_.matmul(
            PB(7)[:, 0:4], ones1_sb[:, :], trow[0:1, 0:4], start=True, stop=True), inc=tT)
        E.wait("vector", tT, tb)
        vtc = E.op("vector", lambda E_: E_.tensor_copy(out=thr_col[:, :], in_=PB(7)[:, 0:4]), inc=tV)
        S["b7V"] = vtc
        E.wait("vector", tV, vtc)
        tern_state["thr_v"] = vtc

    def tern_pass_unit(pos):
        idx = TORDER[pos]
        src, dst, wid, tc = tern_src(idx)
        par = pos % 2
        buf = wstage[par]
        dl = dTlp[par]
        eng = "gpsimd" if par == 0 else "sync"
        res = wtern if par == 0 else wtern2
        dw = dTw if par == 0 else dTw2
        E.wait(eng, tG, tern_state["last_g"][TIDX + pos - 2 if pos >= 2
                                             else TIDX - 2 + pos])
        E.op(eng, lambda E_, src=src, w=wid, b=buf: E_.dma_start(
            out=b[:, 0:w], in_=src), inc=dl)
        E.wait("vector", dl, dl.n)
        E.wait("vector", tV, tern_state["thr_v"])
        g1 = E.op("vector", lambda E_, w=wid, tc=tc, b=buf: E_.tensor_single_scalar(
            out=t2buf[:, 0:w], in_=b[:, 0:w],
            scalar=thr_col[:, tc + 1:tc + 2], op=ALU.is_lt), inc=tG)
        E.wait("vector", tG, g1)
        E.wait("vector", dw, tern_state["wr"][par])
        g2 = E.op("vector", lambda E_, w=wid, tc=tc, b=buf, res=res: E_.scalar_tensor_tensor(
            out=res[:, 0:w], in0=b[:, 0:w], scalar=thr_col[:, tc:tc + 1],
            in1=t2buf[:, 0:w], op0=ALU.is_gt, op1=ALU.subtract), inc=tG)
        tern_state["last_g"][TIDX + pos] = g2
        E.wait(eng, tG, g2)
        E.op(eng, lambda E_, dst=dst, w=wid, res=res: E_.dma_start(
            out=dst, in_=res[:, 0:w]), inc=dw)
        tern_state["wr"][par] = dw.n

    tern_iter = iter(range(TIDX))
    tern_iter2 = iter(range(TIDX))
    qkv_written = {}
    stats_cast_done = {}

    def emit_stats(s):
        E.wait("scalar", dXp[s % 2], XP1[s])

        def cast1(ec, t_free, v_free, s=s):
            E.wait("scalar", tT, t_free)
            E.wait("scalar", tV, v_free)
            v = E.op("scalar", lambda E_, ec=ec, s=s: E_.copy(
                out=xb2_b[ec % 2][:, :], in_=x_sl[s % 2][:, ec, :]), inc=tS)
            stats_cast_done[s] = v
            return v
        emit_ln_stats(lambda ec: xb2_b[ec % 2][:, :], cast1, stats1_sb, s)

    def emit_bcast_xln(s):
        vrb = emit_bcast(stats1_sb, s)
        E.wait("vector", dXp[s % 2], XP1[s])
        E.wait("vector", tV, vrb)
        xln_sl = xln_sl2[s % 2]
        for ec in range(EC):
            E.op("vector", lambda E_, ec=ec, s=s: E_.tensor_tensor(
                out=tmp_sl[:, :], in0=x_sl[s % 2][:, ec, :], in1=rb_sb[:, :], op=ALU.mult))
            last = E.op("vector", lambda E_, ec=ec, xln_sl=xln_sl: E_.tensor_tensor(
                out=xln_sl[:, ec, :], in0=tmp_sl[:, :], in1=mrb_sb[:, :], op=ALU.subtract),
                inc=(tV if ec == EC - 1 else None))
        S[f"xlnv{s}"] = last

    E.op("sync", lambda E_: E_.dma_start(
        out=x_sl[1][:, :, :], in_=x_d[1, :, :, :]), inc=dXp[1])
    XP1[1] = dXp[1].n
    emit_stats(0)
    emit_bcast_xln(0)
    E.wait("tensor", dW, W_LOADS)
    proj_evac = {}
    proj_un = [0]
    stage_dq = {}
    for s in range(NSLAB):
        xln_sl = xln_sl2[s % 2]
        vxln = S[f"xlnv{s}"]
        if s + 2 < NSLAB:
            E.wait("sync", tV, vxln)
            E.wait("sync", tS, stats_cast_done[s])
            E.op("sync", lambda E_, s=s: E_.dma_start(
                out=x_sl[s % 2][:, :, :], in_=x_d[s + 2, :, :, :]), inc=dXp[s % 2])
            XP1[s + 2] = dXp[s % 2].n
        E.op("scalar", lambda E_: _table_load(E_, set_silu, nc))
        E.wait("tensor", tV, vxln)
        for mi, (mat, wsb) in enumerate((("q", wq_sb), ("k", wk_sb), ("v", wv_sb))):
            for oc in range(EC):
                un = proj_un[0]
                bank = un % 2
                if un >= 2:
                    E.wait("tensor", tS, proj_evac[un - 2])
                for kc in range(EC):
                    last = E.op("tensor", lambda E_, w=wsb, oc=oc, kc=kc, bank=bank, xln_sl=xln_sl: E_.matmul(
                        PB(bank)[:, :], w[:, kc, 128 * oc:128 * (oc + 1)], xln_sl[:, kc, :],
                        start=(kc == 0), stop=(kc == EC - 1)),
                        inc=(tT if kc == EC - 1 else None))
                E.wait("scalar", tT, last)
                stg = qst_b[un % 8]
                if un >= 8:
                    E.wait("scalar", dQ, stage_dq[un - 8])
                if mat == "v":
                    sev = E.op("scalar", lambda E_, st=stg, bank=bank: E_.copy(
                        out=st[:, :], in_=PB(bank)[:, :]), inc=tS)
                else:
                    sev = E.op("scalar", lambda E_, st=stg, bank=bank: E_.activation(
                        out=st[:, :], in_=PB(bank)[:, :], func=ACT.Silu), inc=tS)
                proj_evac[un] = sev
                if mat == "q":
                    dst = qks_d[s, :, oc, 0, :]
                elif mat == "k":
                    dst = qks_d[s, :, oc, 1, :]
                else:
                    dst = vs_d[s, :, oc, :]
                E.op("scalar", lambda E_, d=dst, st=stg: E_.dma_start(out=d, in_=st[:, :]), inc=dQ)
                stage_dq[un] = dQ.n
                proj_un[0] = un + 1
        qkv_written[s] = dQ.n
        E.wait("tensor", tV, S["b7V"])
        E.wait("tensor", tS, S["b7S"])
        for kc in range(EC):
            last = E.op("tensor", lambda E_, kc=kc, xln_sl=xln_sl: E_.matmul(
                PB(7)[0:64, :], wlb_sb[:, kc, :], xln_sl[:, kc, :],
                start=(kc == 0), stop=(kc == EC - 1)), inc=(tT if kc == EC - 1 else None))
        if s == NSLAB - 1:
            S["lastproj_t"] = last
        E.wait("scalar", tT, last)
        stnh = E.op("scalar", lambda E_: E_.activation(
            out=tmp_sl[0:64, :], in_=PB(7)[0:64, :], func=ACT.Tanh, scale=0.5), inc=tS)
        S["b7S"] = stnh
        E.wait("vector", tS, stnh)
        E.op("vector", lambda E_: E_.tensor_scalar(
            out=tmp_sl[64:128, :], in0=tmp_sl[0:64, :], scalar1=0.5, scalar2=0.5,
            op0=ALU.mult, op1=ALU.add))
        vrd = E.op("vector", lambda E_, s=s: E_.reduce_sum(
            out=lrwdc_all[:, s, :], in_=tmp_sl[64:128, :].rearrange("p (a b) -> p a b", b=CHUNK),
            axis=AX.X), inc=tV)
        E.wait("vector", tV, vrd)
        E.op("vector", lambda E_, s=s: E_.tensor_scalar(
            out=lrwdc_all[0:32, s, :], in0=lrwdc_all[0:32, s, :], scalar1=BASE_LR / CHUNK,
            scalar2=None, op0=ALU.mult))
        vv = E.op("vector", lambda E_, s=s: E_.tensor_scalar(
            out=lrwdc_all[32:64, s, :], in0=lrwdc_all[32:64, s, :], scalar1=BASE_WD / CHUNK,
            scalar2=None, op0=ALU.mult), inc=tV)
        S[f"lrwd{s}"] = vv
        if s + 1 < NSLAB:
            E.op("scalar", lambda E_: _table_load(E_, set_exp, nc))
            emit_stats(s + 1)
            emit_bcast_xln(s + 1)
        if s < 4:
            for _ in range(10):
                nxt = next(tern_iter, None)
                if nxt is not None:
                    tern_scale_unit(nxt)
            if s == 3:
                tern_thresholds()
        else:
            for _ in range(10):
                nxt = next(tern_iter2, None)
                if nxt is not None:
                    tern_pass_unit(nxt)
    TERN_W = dTw.n
    TERN_W2 = dTw2.n
    P1_T, P1_V, P1_S = tT.n, tV.n, tS.n
    _ckpt("p1b")

    LASTPROJ = S["lastproj_t"]
    E.wait("gpsimd", tT, LASTPROJ)
    gdma(wo_sb[:, :, :], wo_d[:, :, :].rearrange("e p m -> p e m"))
    WO_L = dW.n
    E.wait("scalar", tT, LASTPROJ)
    for s0 in (0, 1):
        E.wait("scalar", dQ, qkv_written[s0])
        E.op("scalar", lambda E_, s0=s0: E_.dma_start(
            out=qk_sl[s0 % 2][:, :, :, :], in_=qks_d[s0, :, :, :, :]), inc=dQp[s0 % 2])
        E.op("scalar", lambda E_, s0=s0: E_.dma_start(
            out=v_sl[s0 % 2][:, :, :], in_=vs_d[s0, :, :, :]), inc=dQp[s0 % 2])
        S[f"p2ldqk{s0}"] = dQp[s0 % 2].n

    E.op("scalar", lambda E_: _table_load(E_, set_exp, nc))
    P1_G = tG.n
    for eng in ("sync", "gpsimd", "scalar", "vector", "tensor"):
        E.wait(eng, tT, P1_T)
        E.wait(eng, tV, P1_V)
        E.wait(eng, tS, P1_S)
        E.wait(eng, tG, P1_G)
        E.wait(eng, dTw, TERN_W)
        E.wait(eng, dTw2, TERN_W2)
        E.wait(eng, dTlp[0], dTlp[0].n)
        E.wait(eng, dTlp[1], dTlp[1].n)
        E.wait(eng, dXp[0], dXp[0].n)
        E.wait(eng, dXp[1], dXp[1].n)
        E.wait(eng, dQ, dQ.n)
        E.wait(eng, dX, dX.n)
    E.wait("tensor", dW, WO_L)
    for buf in (ktr[0], ktr[1], vtr[0], vtr[1]):
        mz = E.op("vector", lambda E_, b=buf: E_.memset(b[:, :, :], 0.0), inc=tV)
    S["repz"] = mz

    def p2_loads_qk(s):
        if s < 2:
            return
        E.wait("sync", dQ, qkv_written[s])
        if s >= 2:
            E.wait("sync", tT, S["lastuse_qk"][s - 2])
        E.op("sync", lambda E_, s=s: E_.dma_start(
            out=qk_sl[s % 2][:, :, :, :], in_=qks_d[s, :, :, :, :]), inc=dQp[s % 2])
        E.op("sync", lambda E_, s=s: E_.dma_start(
            out=v_sl[s % 2][:, :, :], in_=vs_d[s, :, :, :]), inc=dQp[s % 2])
        S[f"p2ldqk{s}"] = dQp[s % 2].n

    def p2_loads_x(s):
        if s >= 2:
            E.wait("sync", tV, S["xev"].get(s - 2, 0))
        E.op("sync", lambda E_, s=s: E_.dma_start(
            out=x_sl[s % 2][:, :, :], in_=x_d[s, :, :, :]), inc=dXp[s % 2])
        S[f"p2ldx{s}"] = dXp[s % 2].n

    S["lastuse_qk"] = {}

    def p2_maps(s):
        E.wait("vector", tV, S[f"lrwd{s}"])
        if s >= 1:
            E.wait("vector", tV, S["maps_v"][s - 1])
            E.wait("vector", tG, S["pw"][s * NCHS - 1])
        vsc = E.op("vector", lambda E_, s=s: E_.tensor_tensor(
            out=scat_sb[:, :, :],
            in0=maskp_sb[:, :].unsqueeze(2).broadcast_to([64, NP, NCHS]),
            in1=lrwdc_all[:, s, :].unsqueeze(1).broadcast_to([64, NP, NCHS]),
            op=ALU.mult), inc=tV)
        E.wait("tensor", tV, vsc)
        E.wait("tensor", tV, S["b7V"])
        E.wait("tensor", tS, S["b7S"])
        for m in range(4):
            last = E.op("tensor", lambda E_, m=m: E_.matmul(
                PB(7)[:, m * NP * NCHS:(m + 1) * NP * NCHS],
                selm_sb[:, m, :], scat_sb[:, :, :].rearrange("p a b -> p (a b)"),
                start=True, stop=True), inc=(tT if m == 3 else None))
        E.wait("vector", tT, last)
        S["maps_v"][s] = E.op("vector", lambda E_: E_.tensor_copy(
            out=maps_sb[:, :, :, :].rearrange("p a b c -> p (a b c)"),
            in_=PB(7)[:, 0:4 * NP * NCHS]), inc=tV)
        S["b7V"] = S["maps_v"][s]

    def kvt_subunits(s1, j, subs, rep_gate):
        ptH = PB(2)[:, :].bitcast(BF16)
        for (is_k, half) in subs:
            dstt = kt_sl if is_k else vt_sl
            E.wait("tensor", dQp[s1 % 2], S[f"p2ldqk{s1}"])
            E.wait("tensor", dR, rep_gate)
            E.wait("tensor", tS, S["kvt_ev"])
            E.wait("tensor", dTw, TERN_W)
            E.wait("tensor", dTw2, TERN_W2)
            for pq in range(4):
                pair = half * 4 + pq
                if is_k:
                    in_ap = qk_sl[s1 % 2][:, pair, 1, 128 * j:128 * (j + 1)]
                else:
                    in_ap = v_sl[s1 % 2][:, pair, 128 * j:128 * (j + 1)]
                last = E.op("tensor", lambda E_, i_=in_ap, pq=pq: E_.transpose(
                    ptH[:, 512 + 128 * pq:512 + 128 * (pq + 1)], i_, ident_b[:, :]),
                    inc=(tT if pq == 3 else None))
            E.wait("scalar", tT, last)
            S["kvt_ev"] = E.op("scalar", lambda E_, d=dstt, j=j, h=half: E_.copy(
                out=d[:, j, 4 * h:4 * (h + 1), :],
                in_=ptH[:, 512:1024]), inc=tS)
        return S["kvt_ev"]

    def p2_kvt_full(s):
        E.wait("scalar", dR, dR.n)
        for j in range(SLAB // 128):
            S["kvt"][s] = kvt_subunits(
                s, j, [(True, 0), (True, 1), (False, 0), (False, 1)], dR.n)

    def emit_chunk(c, tA=None, tMid=None, tTail=None):
        s = c // NCHS
        i = c % NCHS
        cp = i // 2
        half = (i % 2) * 64
        cur = S["cur"]
        nxt = 1 - cur
        rb = c % 2
        sb = s % 2
        ptB2 = PB(2)[:, :].bitcast(BF16)
        b0v = PB(0)[:, :].rearrange("p (q n) -> p q n", n=DH)
        b1v = PB(1)[:, :].rearrange("p (q n) -> p q n", n=DH)
        b6v = PB(6)[:, :].rearrange("p (q n) -> p q n", n=DH)
        aTv = aTs_sb[:, :, :].rearrange("p g (j n) -> p j g n", n=64)
        ptPv = ptB2[:, 0:512].rearrange("p (j g n) -> p j g n", j=4, n=64)
        ptSv = pt_sb[:, :, :].rearrange("p j (g n) -> p j g n", n=64)

        def mapsl(m, i):
            return maps_sb[:, m, :, i].unsqueeze(2).broadcast_to([128, NP, DH])

        E.wait("gpsimd", tV, S["maps_v"][s])
        if c == 0:
            E.wait("gpsimd", dW, W_CONST)
        if c >= 1:
            E.wait("gpsimd", tV, S["u3b"][c - 1])
        E.op("gpsimd", lambda E_, i=i, cur=cur: E_.tensor_tensor(
            out=wdec_s[:, :, :], in0=win_f[cur][:, :, :], in1=mapsl(2, i), op=ALU.mult))
        E.op("gpsimd", lambda E_, i=i, cur=cur: E_.tensor_tensor(
            out=wdec1_s[:, :, :], in0=wo1_f[cur][:, :, :], in1=mapsl(3, i), op=ALU.mult))
        S["pw"][c] = E.op("gpsimd", lambda E_, i=i, cur=cur: E_.tensor_tensor(
            out=wdec2_s[:, :, :], in0=wo2_f[cur][:, :, :], in1=mapsl(3, i), op=ALU.mult), inc=tG)

        E.wait("scalar", tS, S["kvt"][s])
        if c < 2:
            E.wait("scalar", tV, S["repz"])
        if c >= 2:
            E.wait("scalar", tT, S["Gin"][c - 2])
        for (src, dst) in ((kt_sl, ktr[rb]), (vt_sl, vtr[rb])):
            for par in range(2):
                E.op("scalar", lambda E_, s_=src, d=dst, cp=cp, hh=half, par=par: E_.dma_start(
                    out=d[64 * par:64 * (par + 1), :, :].rearrange(
                        "p (a b) c -> p a b c", b=2)[:, :, par, :],
                    in_=s_[hh:hh + 64, cp, :, :].rearrange(
                        "p (a b) c -> p a b c", b=2)[:, :, par, :]), inc=dR)
        S["rep"][c] = dR.n
        _ckpt(f"c{c}:rep")

        if c == 0:
            E.wait("tensor", tS, S["initcast"])
            E.wait("tensor", tV, S["repz"])
        else:
            E.wait("tensor", tV, S["u3b"][c - 1])
            E.wait("tensor", tS, S["exp_k"][c - 1])
        if i == 0:
            E.wait("tensor", dQp[s % 2], S[f"p2ldqk{s}"])
        for pair in range(NP):
            for ch in range(2):
                last = E.op("tensor", lambda E_, p=pair, ch=ch, i=i, sb=sb: E_.matmul(
                    PB(1)[64 * ch:64 * (ch + 1), 64 * p:64 * (p + 1)],
                    win_b[64 * ch:64 * (ch + 1), p, :],
                    qk_sl[sb][64 * ch:64 * (ch + 1), p, 1, CHUNK * i:CHUNK * (i + 1)],
                    start=True, stop=True, tile_position=(64 * ch, 64 * ch)),
                    inc=(tT if (pair == NP - 1 and ch == 1) else None))
        S["mm1k"][c] = last
        if c >= 1:
            E.wait("tensor", tS, S["exp_q"][c - 1])
        for pair in range(NP):
            for ch in range(2):
                last = E.op("tensor", lambda E_, p=pair, ch=ch, i=i, sb=sb: E_.matmul(
                    PB(0)[64 * ch:64 * (ch + 1), 64 * p:64 * (p + 1)],
                    win_b[64 * ch:64 * (ch + 1), p, :],
                    qk_sl[sb][64 * ch:64 * (ch + 1), p, 0, CHUNK * i:CHUNK * (i + 1)],
                    start=True, stop=True, tile_position=(64 * ch, 64 * ch)),
                    inc=(tT if (pair == NP - 1 and ch == 1) else None))
        S["mm1q"][c] = last
        _ckpt(f"c{c}:mm1")

        E.wait("tensor", tS, S["cast2"])
        if c == 0:
            E.wait("tensor", tV, S["repz"])
        if c >= 1:
            E.wait("tensor", tS, S["aTev"][c - 1])
        for pair in range(NP):
            pp = pair % 2
            j = pair // 2
            for ch in range(2):
                last = E.op("tensor", lambda E_, p=pair, ch=ch, pp=pp, j=j, i=i, sb=sb: E_.matmul(
                    PB(3 + ch)[64 * pp:64 * (pp + 1), 64 * j:64 * (j + 1)],
                    v_sl[sb][64 * ch:64 * (ch + 1), p, CHUNK * i:CHUNK * (i + 1)],
                    wo2_b[64 * ch:64 * (ch + 1), p, :],
                    start=True, stop=True, tile_position=(64 * ch, 64 * pp)),
                    inc=(tT if (pair == NP - 1 and ch == 1) else None))
        S["aT"][c] = last
        _ckpt(f"c{c}:aT")

        if tA is not None:
            tA()

        E.wait("scalar", tT, S["mm1k"][c])
        S["exp_k"][c] = E.op("scalar", lambda E_: E_.activation(
            out=e_sb[:, 1, :], in_=PB(1)[:, :], func=ACT.Exp), inc=tS)
        E.wait("scalar", tT, S["mm1q"][c])
        S["exp_q"][c] = E.op("scalar", lambda E_: E_.activation(
            out=e_sb[:, 0, :], in_=PB(0)[:, :], func=ACT.Exp), inc=tS)
        E.wait("scalar", tT, S["aT"][c])
        if c >= 1:
            E.wait("scalar", tV, S["u"][c - 1])
        E.op("scalar", lambda E_: E_.copy(
            out=aTs_sb[:, 0, :], in_=b34[:, 0:256]))
        S["aTev"][c] = E.op("scalar", lambda E_: E_.copy(
            out=aTs_sb[:, 1, :], in_=b34[:, 512:768]), inc=tS)

        E.wait("vector", tS, S["exp_k"][c])
        v1 = E.op("vector", lambda E_: E_.reduce_sum(
            out=sums_sb[:, 8:16], in_=e_sb[:, 1, :].rearrange("p (q n) -> p q n", n=64),
            axis=AX.X), inc=tV)
        E.wait("vector", tV, v1)
        v2 = E.op("vector", lambda E_: E_.reciprocal(
            out=rcp_sb[:, 8:16], in_=sums_sb[:, 8:16]), inc=tV)
        E.wait("vector", tV, v2)
        S["pk"][c] = E.op("vector", lambda E_: E_.tensor_tensor(
            out=p_sb[:, 1, :].rearrange("p (q n) -> p q n", n=64),
            in0=e_sb[:, 1, :].rearrange("p (q n) -> p q n", n=64),
            in1=rcp_sb[:, 8:16].unsqueeze(2).broadcast_to([128, NP, 64]),
            op=ALU.mult), inc=tV)
        E.wait("vector", tS, S["exp_q"][c])
        v1 = E.op("vector", lambda E_: E_.reduce_sum(
            out=sums_sb[:, 0:8], in_=e_sb[:, 0, :].rearrange("p (q n) -> p q n", n=64),
            axis=AX.X), inc=tV)
        E.wait("vector", tV, v1)
        v2 = E.op("vector", lambda E_: E_.reciprocal(
            out=rcp_sb[:, 0:8], in_=sums_sb[:, 0:8]), inc=tV)
        E.wait("vector", tV, v2)
        S["pq"][c] = E.op("vector", lambda E_: E_.tensor_tensor(
            out=p_sb[:, 0, :].rearrange("p (q n) -> p q n", n=64),
            in0=e_sb[:, 0, :].rearrange("p (q n) -> p q n", n=64),
            in1=rcp_sb[:, 0:8].unsqueeze(2).broadcast_to([128, NP, 64]),
            op=ALU.mult), inc=tV)
        _ckpt(f"c{c}:soft")

        E.wait("tensor", tV, S["pk"][c])
        if c >= 1:
            E.wait("tensor", tS, S["ptevac"][c - 1])
        for j in range(4):
            in_ap = p_sb[:, 1, 128 * j:128 * (j + 1)]
            last = E.op("tensor", lambda E_, i_=in_ap, j=j: E_.transpose(
                ptB2[:, 128 * j:128 * (j + 1)], i_, ident_b[:, :]),
                inc=(tT if j == 3 else None))
        S["pT"][c] = last
        _ckpt(f"c{c}:pT")

        E.wait("scalar", tT, S["pT"][c])
        S["ptevac"][c] = E.op("scalar", lambda E_: E_.copy(
            out=pt_sb[:, :, :].rearrange("p a b -> p (a b)"), in_=ptB2[:, 0:512]), inc=tS)

        E.wait("tensor", tS, S["ptevac"][c])
        E.wait("tensor", dR, S["rep"][c])
        if c >= 1:
            E.wait("tensor", tV, S["gs1"][c - 1])
        for pair in range(NP):
            j = pair // 2
            for ch in range(2):
                last = E.op("tensor", lambda E_, p=pair, j=j, ch=ch, rb=rb: E_.matmul(
                    b0v[64 * ch:64 * (ch + 1), p, :],
                    pt_sb[:, j, 64 * ch:64 * (ch + 1)],
                    vtr[rb][:, p, 64 * ch:64 * (ch + 1)],
                    start=True, stop=True, tile_position=(0, 64 * ch)),
                    inc=(tT if (pair == NP - 1 and ch == 1) else None))
        S["G1"][c] = last
        if c >= 1:
            E.wait("tensor", tV, S["gs2"][c - 1])
        for pair in range(NP):
            j = pair // 2
            for ch in range(2):
                last = E.op("tensor", lambda E_, p=pair, j=j, ch=ch, rb=rb: E_.matmul(
                    b1v[64 * ch:64 * (ch + 1), p, :],
                    vtr[rb][:, p, 64 * ch:64 * (ch + 1)],
                    pt_sb[:, j, 64 * ch:64 * (ch + 1)],
                    start=True, stop=True, tile_position=(0, 64 * ch)),
                    inc=(tT if (pair == NP - 1 and ch == 1) else None))
        S["G2"][c] = last
        _ckpt(f"c{c}:G12")

        if tMid is not None:
            tMid()

        E.wait("vector", tS, S["ptevac"][c])
        E.wait("vector", tS, S["aTev"][c])
        S["pa_v"][c] = E.op("vector", lambda E_: E_.tensor_tensor(
            out=pa_sb[:, :, :, :], in0=ptSv, in1=aTv, op=ALU.mult), inc=tV)
        v3 = E.op("vector", lambda E_: E_.reduce_sum(
            out=s_sb[:, :], in_=pa_sb[:, :, :, :].rearrange("p j g n -> p (j g) n"),
            axis=AX.X), inc=tV)
        E.wait("vector", tV, v3)
        S["u"][c] = E.op("vector", lambda E_: E_.tensor_tensor(
            out=u_sb[:, :, :, :],
            in0=aTv,
            in1=s_sb[:, :].rearrange("p (j g) -> p j g", g=2).unsqueeze(3).broadcast_to([128, 4, 2, 64]),
            op=ALU.subtract), inc=tV)
        S["gz"][c] = E.op("vector", lambda E_: E_.tensor_tensor(
            out=gz_sb[:, :, :].rearrange("p j (g n) -> p j g n", n=64),
            in0=ptSv, in1=u_sb[:, :, :, :], op=ALU.mult), inc=tV)
        _ckpt(f"c{c}:gz")

        E.wait("tensor", tV, S["gz"][c])
        if c >= 1:
            E.wait("tensor", tV, max(S["gs3"][c - 1], S["b6_free"]))
        else:
            E.wait("tensor", tV, S["b6_free"])
        for pair in range(NP):
            j = pair // 2
            for ch in range(2):
                last = E.op("tensor", lambda E_, p=pair, j=j, ch=ch, rb=rb: E_.matmul(
                    b6v[64 * ch:64 * (ch + 1), p, :],
                    ktr[rb][:, p, 64 * ch:64 * (ch + 1)],
                    gz_sb[:, j, 64 * ch:64 * (ch + 1)],
                    start=True, stop=True, tile_position=(0, 64 * ch)),
                    inc=(tT if (pair == NP - 1 and ch == 1) else None))
        S["Gin"][c] = last
        _ckpt(f"c{c}:Gin")

        E.wait("tensor", tV, S["pq"][c])
        E.wait("tensor", tS, S["cast1"])
        if c >= 1:
            E.wait("tensor", tS, S["oevac"][c - 1])
        E.wait("tensor", tV, S["b5_free"])
        for pair in range(NP):
            for ch in range(2):
                last = E.op("tensor", lambda E_, p=pair, ch=ch: E_.matmul(
                    PB(5)[64 * ch:64 * (ch + 1), 64 * p:64 * (p + 1)],
                    wo1_b[64 * ch:64 * (ch + 1), p, :],
                    p_sb[64 * ch:64 * (ch + 1), 0, 64 * p:64 * (p + 1)],
                    start=True, stop=True, tile_position=(64 * ch, 64 * ch)),
                    inc=(tT if (pair == NP - 1 and ch == 1) else None))
        S["o"][c] = last
        E.wait("scalar", tT, S["o"][c])
        S["oevac"][c] = E.op("scalar", lambda E_, i=i, sb=sb: E_.copy(
            out=o_sl[sb][:, :, CHUNK * i:CHUNK * (i + 1)],
            in_=PB(5)[:, :]), inc=tS)
        _ckpt(f"c{c}:o")

        if tTail is not None:
            tTail()

        E.wait("vector", tT, S["G1"][c])
        if c >= 1:
            E.wait("vector", tG, S["u1f"][c - 1])
        S["gs1"][c] = E.op("vector", lambda E_, i=i: E_.tensor_tensor(
            out=gs_s1[:, :, :], in0=b0v, in1=mapsl(1, i), op=ALU.mult), inc=tV)
        E.wait("vector", tT, S["G2"][c])
        if c >= 1:
            E.wait("vector", tG, S["u2f"][c - 1])
        S["gs2"][c] = E.op("vector", lambda E_, i=i: E_.tensor_tensor(
            out=gs_s2[:, :, :], in0=b1v, in1=mapsl(1, i), op=ALU.mult), inc=tV)
        E.wait("vector", tT, S["Gin"][c])
        if c >= 1:
            E.wait("vector", tG, S["u3f"][c - 1])
        S["gs3"][c] = E.op("vector", lambda E_, i=i: E_.tensor_tensor(
            out=gs_s[:, :, :], in0=b6v, in1=mapsl(0, i), op=ALU.mult), inc=tV)
        E.wait("vector", tG, S["pw"][c])
        S["u3b"][c] = E.op("vector", lambda E_: E_.tensor_tensor(
            out=win_b[:, :, :], in0=wdec_s[:, :, :], in1=gs_s[:, :, :], op=ALU.add), inc=tV)

        E.wait("gpsimd", tV, S["gs1"][c])
        S["u1f"][c] = E.op("gpsimd", lambda E_, nxt=nxt: E_.tensor_tensor(
            out=wo1_f[nxt][:, :, :], in0=wdec1_s[:, :, :], in1=gs_s1[:, :, :], op=ALU.add), inc=tG)
        E.wait("gpsimd", tV, S["gs2"][c])
        S["u2f"][c] = E.op("gpsimd", lambda E_, nxt=nxt: E_.tensor_tensor(
            out=wo2_f[nxt][:, :, :], in0=wdec2_s[:, :, :], in1=gs_s2[:, :, :], op=ALU.add), inc=tG)
        E.wait("gpsimd", tV, S["gs3"][c])
        S["u3f"][c] = E.op("gpsimd", lambda E_, nxt=nxt: E_.tensor_tensor(
            out=win_f[nxt][:, :, :], in0=wdec_s[:, :, :], in1=gs_s[:, :, :], op=ALU.add), inc=tG)

        E.wait("scalar", tG, S["u1f"][c])
        E.wait("scalar", tT, S["o"][c])
        S["cast1"] = E.op("scalar", lambda E_, nxt=nxt: E_.copy(
            out=wo1_b[:, :, :], in_=wo1_f[nxt][:, :, :]), inc=tS)
        E.wait("scalar", tG, S["u2f"][c])
        S["cast2"] = E.op("scalar", lambda E_, nxt=nxt: E_.copy(
            out=wo2_b[:, :, :], in_=wo2_f[nxt][:, :, :]), inc=tS)
        S["cur"] = nxt
        if i == NCHS - 1:
            S["lastuse_qk"][s] = S["aT"][c]

    def wo_mms(s, ec, first):
        sb = s % 2
        if first and ec == 0:
            E.wait("tensor", tS, S["oevac"][(s + 1) * NCHS - 1])
            E.wait("tensor", dW, WO_L)
        if first:
            E.wait("tensor", tV, S["b7V"])
        ks = range(0, EC // 2) if first else range(EC // 2, EC)
        for k in ks:
            last = E.op("tensor", lambda E_, ec=ec, k=k, sb=sb: E_.matmul(
                PB(7)[:, :], wo_sb[:, k, 128 * ec:128 * (ec + 1)], o_sl[sb][:, k, :],
                start=(k == 0), stop=(k == EC - 1)),
                inc=(tT if k == EC - 1 else None))
        if not first:
            S["womm"] = last

    def wo_xev(s, ec, xev_l):
        sb = s % 2
        if ec == 0:
            E.wait("vector", dXp[s % 2], S[f"p2ldx{s}"])
            if s >= 1:
                E.wait("vector", dX2, 16 * s)
            E.wait("vector", dTw, TERN_W)
            E.wait("vector", dTw2, TERN_W2)
        E.wait("vector", tT, S["womm"])
        xev = E.op("vector", lambda E_, ec=ec, sb=sb: E_.tensor_tensor(
            out=x2_sl[:, ec, :], in0=PB(7)[:, :], in1=x_sl[sb][:, ec, :], op=ALU.add), inc=tV)
        S["b7V"] = xev
        xev_l.append(xev)
        if ec == EC - 1:
            S["xev"][s] = xev

    def p2_stats_store(s, xev_l, last_c):
        S["b6_free"] = max(S["b6_free"], S["gs3"][last_c])
        E.wait("tensor", tS, S["oevac"][last_c])
        E.wait("sync", tV, xev_l[-1])
        E.op("sync", lambda E_, s=s: E_.dma_start(out=x2s_d[s, :, :, :], in_=x2_sl[:, :, :]), inc=dX2)

        def cast2(ec, t_free, v_free):
            E.wait("scalar", tT, t_free)
            E.wait("scalar", tV, max(v_free, xev_l[ec]))
            return E.op("scalar", lambda E_, ec=ec: E_.copy(
                out=xb2_b[ec % 2][:, :], in_=x2_sl[:, ec, :]), inc=tS)
        emit_ln_stats(lambda ec: xb2_b[ec % 2][:, :], cast2, stats2_sb, s)

    p2_loads_qk(0)
    p2_loads_x(0)
    p2_maps(0)
    p2_kvt_full(0)
    for s in range(NSLAB):
        if s + 1 < NSLAB:
            p2_loads_qk(s + 1)
        xev_cur = []
        for i in range(NCHS):
            c = s * NCHS + i
            tA = (lambda i=i: wo_mms(s - 1, i, True)) if s >= 1 else None
            tMid = None
            tTail0 = (lambda i=i: wo_mms(s - 1, i, False)) if s >= 1 else None
            if s + 1 < NSLAB and i % 2 == 1:
                j = i // 2

                def tMid(s=s, j=j):
                    kvt_subunits(s + 1, j, [(True, 0), (True, 1)], S["rep"][s * NCHS + 2 * j + 1])

                def tTail(s=s, j=j, i=i):
                    last = kvt_subunits(s + 1, j, [(False, 0), (False, 1)],
                                        S["rep"][s * NCHS + 2 * j + 1])
                    if j == 3:
                        S["kvt"][s + 1] = last
                    if s >= 1:
                        wo_mms(s - 1, i, False)
            else:
                tTail = tTail0
            emit_chunk(c, tA=tA, tMid=tMid, tTail=tTail)
            if s >= 1:
                wo_xev(s - 1, i, xev_cur)
            _ckpt(f"chunk:{c}")
        if s >= 1:
            p2_stats_store(s - 1, xev_cur, s * NCHS + NCHS - 1)
        if s + 1 < NSLAB:
            p2_maps(s + 1)
            p2_loads_x(s + 1)
        _ckpt(f"slab:{s}")
    xev_cur = []
    for i in range(NCHS):
        wo_mms(NSLAB - 1, i, True)
        wo_mms(NSLAB - 1, i, False)
        wo_xev(NSLAB - 1, i, xev_cur)
    p2_stats_store(NSLAB - 1, xev_cur, NSLAB * NCHS - 1)
    _ckpt("p2end")
    P2_T, P2_V, P2_S, P2_G = tT.n, tV.n, tS.n, tG.n

    E.op("scalar", lambda E_: _table_load(E_, set_gelu, nc))
    E.wait("sync", tT, P2_T)
    E.wait("sync", tV, P2_V)
    E.wait("sync", tS, P2_S)
    E.wait("sync", tG, P2_G)
    E.wait("sync", dTw, TERN_W)
    E.wait("sync", dTw2, TERN_W2)
    E.wait("sync", dTlp[0], dTlp[0].n)
    E.wait("sync", dTlp[1], dTlp[1].n)
    E.wait("sync", dXp[0], dXp[0].n)
    E.wait("sync", dXp[1], dXp[1].n)
    E.wait("sync", dQp[0], dQp[0].n)
    E.wait("sync", dQp[1], dQp[1].n)
    E.wait("sync", dR, dR.n)
    E.wait("sync", dX2, 16 * NSLAB)
    E.wait("sync", dX, dX.n)
    E.wait("sync", dQ, dQ.n)
    E.wait("sync", dW, dW.n)
    E.op("sync", lambda E_: E_.dma_start(
        out=w1t_sb[:, :, :], in_=w1ts_d[:, :, :].rearrange("e p m -> p e m")), inc=dC)
    E.op("sync", lambda E_: E_.dma_start(
        out=w2t_sb[:, :, :], in_=w2ts_d[:, :, :].rearrange("e p m -> p e m")), inc=dC)
    WT_LOADED = dC.n
    E.wait("tensor", tG, P2_G)
    E.wait("tensor", tV, P2_V)
    E.wait("tensor", tS, P2_S)
    E.wait("vector", tG, P2_G)
    E.wait("vector", dX2, 16 * NSLAB)
    E.wait("vector", dR, dR.n)
    E.wait("vector", dX, dX.n)
    E.wait("vector", dQ, dQ.n)
    E.wait("vector", dTlp[0], dTlp[0].n)
    E.wait("vector", dTlp[1], dTlp[1].n)
    E.wait("vector", dQp[0], dQp[0].n)
    E.wait("vector", dQp[1], dQp[1].n)
    E.wait("vector", dXp[0], dXp[0].n)
    E.wait("vector", dXp[1], dXp[1].n)
    E.wait("scalar", tG, P2_G)
    E.wait("scalar", dX2, 16 * NSLAB)
    E.wait("scalar", dR, dR.n)
    E.wait("scalar", dX, dX.n)
    E.wait("scalar", dQ, dQ.n)
    E.wait("scalar", dTlp[0], dTlp[0].n)
    E.wait("scalar", dTlp[1], dTlp[1].n)

    NCPC = CSLAB // 128
    prev_evac = {}
    xld = {}
    stage_done = {}

    def p3_load(t):
        sA = (t * CSLAB) // SLAB
        hh = t % (SLAB // CSLAB)
        buf = t % 2
        if t >= 2:
            E.wait("sync", tV, prev_evac[t - 2])
        E.op("sync", lambda E_, sA=sA, hh=hh, buf=buf: E_.dma_start(
            out=x2c[buf][:, :, :], in_=x2s_d[sA, :, :, CSLAB * hh:CSLAB * (hh + 1)]),
            inc=dCx[buf])
        xld[t] = dCx[buf].n

    def p3_stage(t):
        sA = (t * CSLAB) // SLAB
        hh = t % (SLAB // CSLAB)
        buf = t % 2
        vc8 = E.op("vector", lambda E_, sA=sA: E_.tensor_copy(
            out=st8b[:, :], in_=stats2_sb[:, sA, :]), inc=tV)
        E.wait("tensor", tV, vc8)
        E.wait("tensor", tV, S["b6_free"])
        ttr = E.op("tensor", lambda E_: E_.transpose(
            PB(6)[:, :].bitcast(BF16)[0:8, 0:128], st8b[:, :], ident_b[:, :]), inc=tT)
        E.wait("vector", tT, ttr)
        vrw = E.op("vector", lambda E_: E_.tensor_copy(
            out=rows2_sb[:, :], in_=PB(6)[:, :].bitcast(BF16)[0:8, 0:128]), inc=tV)
        S["b6_free"] = vrw
        E.wait("tensor", tV, vrw)
        E.wait("tensor", tV, S["b7V"])
        for g in range(NCPC):
            gg = hh * NCPC + g
            E.op("tensor", lambda E_, g=g, gg=gg: E_.matmul(
                PB(7)[:, 128 * g:128 * (g + 1)], pick_sb[:, gg, :], rows2_sb[:, :],
                start=True, stop=True))
            last = E.op("tensor", lambda E_, g=g, gg=gg: E_.matmul(
                PB(7)[:, 256 + 128 * g:256 + 128 * (g + 1)], pick_sb[:, 4 + gg, :],
                rows2_sb[:, :], start=True, stop=True),
                inc=(tT if g == NCPC - 1 else None))
        E.wait("vector", tT, last)
        E.op("vector", lambda E_, buf=buf: E_.tensor_copy(
            out=repmr_c[buf][:, :], in_=PB(7)[:, 0:CSLAB]))
        vrep = E.op("vector", lambda E_, buf=buf: E_.tensor_copy(
            out=repr_c[buf][:, :], in_=PB(7)[:, 256:256 + CSLAB]), inc=tV)
        S["b7V"] = vrep
        E.wait("vector", tV, vrep)
        E.wait("vector", dCx[buf], xld[t])
        for ec in range(EC):
            E.op("vector", lambda E_, ec=ec, buf=buf: E_.tensor_tensor(
                out=tmp_c[:, :], in0=x2c[buf][:, ec, :], in1=repr_c[buf][:, :], op=ALU.mult))
            last = E.op("vector", lambda E_, ec=ec, buf=buf: E_.tensor_tensor(
                out=x2ln_c[buf][:, ec, :], in0=tmp_c[:, :], in1=repmr_c[buf][:, :],
                op=ALU.subtract), inc=(tV if ec == EC - 1 else None))
        stage_done[t] = last

    p3_load(0)
    p3_stage(0)
    _ckpt("p3s0")
    sh = {}
    t2v = {}
    for t in range(NCS):
        buf = t % 2
        E.wait("tensor", tV, stage_done[t])
        E.wait("tensor", dC, WT_LOADED)
        if t + 1 < NCS:
            p3_load(t + 1)
        def ffn2(fc):
            hb = fc % 2
            E.wait("tensor", tS, sh[fc])
            for ec in range(EC):
                last = E.op("tensor", lambda E_, fc=fc, ec=ec, hb=hb: E_.matmul(
                    PB(ec // 2)[:, 256 * (ec % 2):256 * (ec % 2) + CSLAB],
                    w2t_sb[:, fc, 128 * ec:128 * (ec + 1)], h_sb[hb][:, :],
                    start=(fc == 0 and ec % 2 == 0),
                    stop=(fc == NFC - 1 and ec % 2 == 1)),
                    inc=(tT if ec == EC - 1 else None))
            t2v[fc] = last

        for fc in range(NFC):
            hb = fc % 2
            if fc >= 2:
                E.wait("tensor", tS, sh[fc - 2])
            for kc in range(EC):
                last = E.op("tensor", lambda E_, fc=fc, kc=kc, hb=hb, buf=buf: E_.matmul(
                    PB(4 + hb)[:, 0:CSLAB], w1t_sb[:, kc, 128 * fc:128 * (fc + 1)],
                    x2ln_c[buf][:, kc, :], start=(kc == 0), stop=(kc == EC - 1)),
                    inc=(tT if kc == EC - 1 else None))
            t1v = last
            E.wait("scalar", tT, t1v)
            if fc >= 2:
                E.wait("scalar", tT, t2v[fc - 2])
            sh[fc] = E.op("scalar", lambda E_, fc=fc, hb=hb: E_.activation(
                out=h_sb[hb][:, :], in_=PB(4 + hb)[:, 0:CSLAB], func=ACT.Gelu,
                bias=b1_sb[:, fc:fc + 1]), inc=tS)
            if fc >= 1:
                ffn2(fc - 1)
            if fc == 4 and t + 1 < NCS:
                p3_stage(t + 1)
        ffn2(NFC - 1)
        E.wait("vector", tT, t2v[NFC - 1])
        if t >= 2:
            E.wait("vector", dY, 16 * (t - 1))
        for ec in range(EC):
            last = E.op("vector", lambda E_, ec=ec, buf=buf: E_.scalar_tensor_tensor(
                out=outc[buf][:, ec, :],
                in0=PB(ec // 2)[:, 256 * (ec % 2):256 * (ec % 2) + CSLAB],
                scalar=b2_sb[:, ec:ec + 1], in1=x2c[buf][:, ec, :],
                op0=ALU.add, op1=ALU.add), inc=(tV if ec == EC - 1 else None))
        prev_evac[t] = last
        E.wait("tensor", tV, prev_evac[t])
        E.wait("sync", tV, prev_evac[t])
        E.op("sync", lambda E_, t=t, buf=buf: E_.dma_start(
            out=y_d[t, :, :, :], in_=outc[buf][:, :, :]), inc=dY)
    E.wait("sync", dY, dY.n)

    if dump:
        E.frozen = False
        _dvars = dict(
            e_sb=e_sb, p_sb=p_sb, pt_sb=pt_sb, o_sl=o_sl[0],
            **{f'o_c{i}': o_sl[0][:, :, 64*i:64*(i+1)] for i in range(8)}, sums_sb=sums_sb,
            rcp_sb=rcp_sb, gz_sb=gz_sb, pa_sb=pa_sb, u_sb=u_sb,
            s_sb=s_sb, maps_sb=maps_sb, st_sb=st_sb,
            win_f0=win_f[0], win_f1=win_f[1], wo1_f0=wo1_f[0], wo1_f1=wo1_f[1],
            wo2_f0=wo2_f[0], wo2_f1=wo2_f[1], win_b=win_b, wo1_b=wo1_b,
            wo2_b=wo2_b, ktr0=ktr[0], ktr1=ktr[1], vtr0=vtr[0], vtr1=vtr[1],
            x2_sl=x2_sl, stats2_sb=stats2_sb, stats1_sb=stats1_sb,
            thr_col=thr_col, rb_sb=rb_sb, mrb_sb=mrb_sb, rows_sb=rows_sb,
            scat_sb=scat_sb, wtpart=wtpart, lrwdc_all=lrwdc_all,
            kt_sl=kt_sl, vt_sl=vt_sl,
            x2c0=x2c[0], x2ln0=x2ln_c[0], reprc0=repr_c[0], repmrc0=repmr_c[0],
            x2c1=x2c[1], x2ln1=x2ln_c[1], reprc1=repr_c[1], repmrc1=repmr_c[1],
            rows2_sb=rows2_sb,
        )
        dS = sem("dS", 16)
        E.wait("sync", tT, tT.n)
        E.wait("sync", tV, tV.n)
        E.wait("sync", tS, tS.n)
        E.wait("sync", tG, tG.n)
        for name in dump:
            h = _dvars[name]
            shp = list(h.shape)
            dt = h.dtype
            dd = dt_("dump_" + name, shp, dt, kind="ExternalOutput").ap()
            idx = tuple([slice(None)] * len(shp))
            E.op("sync", lambda E_, d=dd, h=h, ix=idx: E_.dma_start(out=d[ix], in_=h[ix]), inc=dS)
        E.wait("sync", dS, dS.n)

    with nc.Block() as block:
        E.replay(block)
    return nc




_CACHE = {}
_LAYOUT = {}


def _host_prep(inputs):
    f32 = np.float32
    Wq = np.asarray(inputs["L0_attn_Wq"], f32)
    Wk = np.asarray(inputs["L0_attn_Wk"], f32)
    Wv = np.asarray(inputs["L0_attn_Wv"], f32)
    Wlr = np.asarray(inputs["L0_attn_Wlr"], f32)
    Wbeta = np.asarray(inputs["L0_attn_Wbeta"], f32)
    Wo = np.asarray(inputs["L0_attn_Wo"], f32)
    Wini = np.asarray(inputs["L0_attn_W_in_init"], f32)[0]
    Wouti = np.asarray(inputs["L0_attn_W_out_init"], f32)[0]
    W1 = np.asarray(inputs["L0_ffn_W1"], f32)
    b1 = np.asarray(inputs["L0_ffn_b1"], f32)
    W2 = np.asarray(inputs["L0_ffn_W2"], f32)
    b2 = np.asarray(inputs["L0_ffn_b2"], f32)

    def lhsT(w, out_w):
        return np.ascontiguousarray(w.T.reshape(w.shape[1] // 128, 128, out_w))

    d = {}
    d["wq"] = lhsT(Wq, DM)
    d["wk"] = lhsT(Wk, DM)
    d["wv"] = lhsT(Wv, DM)
    d["wlb"] = lhsT(np.concatenate([Wlr, Wbeta], axis=0), 64)
    d["wo"] = lhsT(Wo, DM)
    d["w1"] = lhsT(W1, FF)
    d["w2"] = np.ascontiguousarray(W2.T.reshape(NFC, 128, DM))
    d["b1a"] = np.ascontiguousarray(b1.reshape(NFC, 128).T)
    d["b2a"] = np.ascontiguousarray(b2.reshape(EC, 128).T)
    wini = np.zeros((128, NP * DH), f32)
    wo1i = np.zeros((128, NP * DH), f32)
    wo2i = np.zeros((128, NP * DH), f32)
    for cpar in range(2):
        for r in range(DH):
            part = cpar * 64 + r
            for p in range(NP):
                h = 2 * p + cpar
                wini[part, p * DH:(p + 1) * DH] = Wini[:, h, r]
                wo1i[part, p * DH:(p + 1) * DH] = Wouti[r, h, :]
                wo2i[part, p * DH:(p + 1) * DH] = Wouti[:, h, r]
    d["wini"] = wini
    d["wo1i"] = wo1i
    d["wo2i"] = wo2i
    d["idn"] = np.eye(128, dtype=f32)
    sel = np.zeros((64, 4, 128), f32)
    for m, base in enumerate((0, 16, 32, 48)):
        for v in range(base, base + 16):
            head = v - base
            q0 = 64 * (head % 2)
            sel[v, m, q0:q0 + 64] = 1.0
    d["selm"] = np.ascontiguousarray(sel.reshape(64, 4 * 128))
    maskp = np.zeros((64, NP), f32)
    for v in range(64):
        maskp[v, (v % 16) // 2] = 1.0
    d["maskp"] = maskp
    d["ones1"] = np.ones((1, 128), f32)
    d["onesc"] = np.ones((128, 1), f32)
    d["cst"] = np.tile(np.array([[EPS, -0.5, 0.5, 0.0, 1.0, 0, 0, 0]], f32), (128, 1))
    pick = np.zeros((8, 8, 128), f32)
    for g in range(8):
        pick[g, g, :] = 1.0
    d["pick"] = np.ascontiguousarray(pick.reshape(8, 8 * 128))
    return d


def kernel(**inputs):
    L = 4096
    if "nc" not in _CACHE:
        _CACHE["nc"] = build_nc(L)
    nc = _CACHE["nc"]
    shared = _host_prep(inputs)
    x = np.asarray(inputs["L0_attn_x"], np.float32)
    B = x.shape[0]
    NSLAB = L // SLAB
    in_maps = []
    for b in range(B):
        xt = np.ascontiguousarray(x[b].T)
        xa = xt.reshape(EC, 128, NSLAB, SLAB).transpose(2, 1, 0, 3)
        m = dict(shared)
        m["x"] = np.ascontiguousarray(xa)
        in_maps.append(m)
    res = run_bass_kernel_spmd(nc, in_maps, core_ids=list(range(B)))
    outs = []
    for b in range(B):
        y = res.results[b]["y"]
        yt = np.transpose(y, (2, 1, 0, 3)).reshape(DM, L)
        outs.append(yt.T)
    return np.ascontiguousarray(np.stack(outs), dtype=np.float32)


# version 21 (final)
# speedup vs baseline: 1.0128x; 1.0128x over previous
import numpy as np
import ml_dtypes
import concourse.bass as bass
import concourse.mybir as mybir
from concourse.bass_utils import run_bass_kernel_spmd
from concourse.hw_specs import get_activation_tables

F32 = mybir.dt.float32
BF16 = mybir.dt.bfloat16
AX = mybir.AxisListType
ALU = mybir.AluOpType
ACT = mybir.ActivationFunctionType

DM = 1024
H = 16
DH = 64
NP = 8
EC = 8
FF = 4096
NFC = 32
CHUNK = 64
SLAB = 512
CSLAB = 256
BASE_LR = 0.001
BASE_WD = 0.9
EPS = 1e-5
NG = SLAB // 128
NCHS = SLAB // CHUNK


class SemC:
    def __init__(self, handle, step=1):
        self.h = handle
        self.step = step
        self.n = 0


class Emit:
    ENGS = ("sync", "gpsimd", "scalar", "vector", "tensor")

    def __init__(self, nc):
        self.nc = nc
        self.q = {e: [] for e in self.ENGS}
        self.frozen = False

    def wait(self, eng, sem, val):
        if self.frozen:
            return
        if val > 0:
            self.q[eng].append(lambda E, s=sem.h, v=val: E.wait_ge(s, v))

    def op(self, eng, fn, inc=None):
        if self.frozen:
            return 0
        if inc is None:
            self.q[eng].append(fn)
            return None
        inc.n += inc.step
        v = inc.n
        self.q[eng].append(lambda E, f=fn, s=inc: f(E).then_inc(s.h, s.step))
        return v

    def replay(self, block):
        for eng in self.ENGS:
            ops = self.q[eng]

            def make(ops):
                def run(E):
                    for f in ops:
                        f(E)
                return run
            getattr(block, eng)(make(ops))


def _act_set_idx(nc, want):
    tabs = get_activation_tables(nc.m.arch)
    for i, (name, funcs) in enumerate(tabs.items()):
        if all(f in funcs for f in want):
            return i
    raise ValueError(f"no table set with {want}")


def _table_load(E_, idx, nc):
    return E_.add_instruction(mybir.InstLoadActFuncSet(
        name=nc.get_next_instruction_name(), act_func_set_id=idx, ins=[], outs=[]))


def build_nc(L=4096, trunc=None, dbg=False, dump=()):
    NSLAB = L // SLAB
    NCS = L // CSLAB
    NCHT = L // CHUNK
    nc = bass.Bass("TRN2", target_bir_lowering=False, debug=False,
                   detect_race_conditions=False)

    set_exp = _act_set_idx(nc, {ACT.Exp, ACT.Ln, ACT.Copy})
    set_silu = _act_set_idx(nc, {ACT.Silu, ACT.Tanh, ACT.Copy})
    set_gelu = _act_set_idx(nc, {ACT.Gelu, ACT.Copy})

    dt_ = nc.dram_tensor
    x_d = dt_("x", [NSLAB, 128, EC, SLAB], F32, kind="ExternalInput").ap()
    y_d = dt_("y", [NCS, 128, EC, CSLAB], F32, kind="ExternalOutput").ap()
    wq_d = dt_("wq", [EC, 128, DM], F32, kind="ExternalInput").ap()
    wk_d = dt_("wk", [EC, 128, DM], F32, kind="ExternalInput").ap()
    wv_d = dt_("wv", [EC, 128, DM], F32, kind="ExternalInput").ap()
    wlb_d = dt_("wlb", [EC, 128, 64], F32, kind="ExternalInput").ap()
    wo_d = dt_("wo", [EC, 128, DM], F32, kind="ExternalInput").ap()
    w1_d = dt_("w1", [EC, 128, FF], F32, kind="ExternalInput").ap()
    w2_d = dt_("w2", [NFC, 128, DM], F32, kind="ExternalInput").ap()
    b1_d = dt_("b1a", [128, NFC], F32, kind="ExternalInput").ap()
    b2_d = dt_("b2a", [128, EC], F32, kind="ExternalInput").ap()
    wini_d = dt_("wini", [128, NP * DH], F32, kind="ExternalInput").ap()
    wo1i_d = dt_("wo1i", [128, NP * DH], F32, kind="ExternalInput").ap()
    wo2i_d = dt_("wo2i", [128, NP * DH], F32, kind="ExternalInput").ap()
    idn_d = dt_("idn", [128, 128], F32, kind="ExternalInput").ap()
    selm_d = dt_("selm", [64, 4 * 128], F32, kind="ExternalInput").ap()
    maskp_d = dt_("maskp", [64, NP], F32, kind="ExternalInput").ap()
    ones1_d = dt_("ones1", [1, 128], F32, kind="ExternalInput").ap()
    onesc_d = dt_("onesc", [128, 1], F32, kind="ExternalInput").ap()
    cst_d = dt_("cst", [128, 8], F32, kind="ExternalInput").ap()
    pick_d = dt_("pick", [8, 8 * 128], F32, kind="ExternalInput").ap()
    _scr = "ExternalOutput" if dbg else "Internal"
    x2s_d = dt_("x2s", [NSLAB, 128, EC, SLAB], F32, kind=_scr).ap()
    w1ts_d = dt_("w1ts", [EC, 128, FF], BF16, kind=_scr).ap()
    w2ts_d = dt_("w2ts", [NFC, 128, DM], BF16, kind=_scr).ap()
    qks_d = dt_("qks", [NSLAB, 128, NP, 2, SLAB], BF16, kind=_scr).ap()
    vs_d = dt_("vs", [NSLAB, 128, NP, SLAB], BF16, kind=_scr).ap()

    SB_BASE = (nc.sbuf_base + 31) & ~31
    SB_LIMIT = nc.sbuf_top

    ARENA_TOP = SB_LIMIT - 4096
    arena = nc.alloc_sbuf_tensor_at(
        "arena", [128, ARENA_TOP - SB_BASE], mybir.dt.uint8, offset=SB_BASE)

    class SAlloc:

        def __init__(self, base):
            self.off = base

        def t(self, name, shape, dtype):
            per = int(np.prod(shape[1:])) * mybir.dt.size(dtype)
            self.off = (self.off + 31) & ~31
            a0 = self.off - SB_BASE
            _LAYOUT[name] = (self.off, per)
            ap = arena[0:shape[0], a0:a0 + per].bitcast(dtype)
            if len(shape) > 2:
                names = " ".join(f"d{i}" for i in range(1, len(shape)))
                kw = {f"d{i}": shape[i] for i in range(2, len(shape))}
                ap = ap.rearrange(f"p ({names}) -> p {names}", **kw)
            self.off += per
            return ap

    P = SAlloc(SB_BASE)
    ident_b = P.t("ident_b", [128, 128], BF16)
    selm_sb = P.t("selm_sb", [64, 4, 128], F32)
    maskp_sb = P.t("maskp_sb", [64, NP], F32)
    ones1_sb = P.t("ones1_sb", [1, 128], F32)
    onesc_sb = P.t("onesc_sb", [128, 1], F32)
    onesb_sb = P.t("onesb_sb", [128, 1], BF16)
    b1_sb = P.t("b1_sb", [128, NFC], F32)
    b2_sb = P.t("b2_sb", [128, EC], F32)
    stats1_sb = P.t("stats1_sb", [128, NSLAB, 8], F32)
    stats2_sb = P.t("stats2_sb", [128, NSLAB, 8], F32)
    lrwdc_all = P.t("lrwdc_all", [64, NSLAB, NCHS], F32)
    thr_col = P.t("thr_col", [128, 4], F32)
    st8b = P.t("st8b", [128, 8], BF16)
    pick_sb = P.t("pick_sb", [8, 8, 128], BF16)
    st_sb = P.t("st_sb", [128, 16], F32)
    rows_sb = P.t("rows_sb", [8, 128], BF16)
    wtpart = P.t("wtpart", [128, EC + NFC], F32)
    wtot = P.t("wtot", [128, 2], F32)
    trow = P.t("trow", [1, 4], F32)
    cst_sb = P.t("cst_sb", [128, 8], F32)
    nc.const_aps.register(F32, EPS, cst_sb[:, 0:1])
    nc.const_aps.register(F32, -0.5, cst_sb[:, 1:2])
    nc.const_aps.register(F32, 0.5, cst_sb[:, 2:3])
    nc.const_aps.register(F32, 0.0, cst_sb[:, 3:4])
    nc.const_aps.register(F32, 1.0, cst_sb[:, 4:5])
    nc.const_aps.register(BF16, 0.0, cst_sb[:, 5:6].bitcast(BF16)[:, 0:1])
    nc.const_aps.register(BF16, 1.0, cst_sb[:, 5:6].bitcast(BF16)[:, 1:2])
    PERS_END = P.off

    Sh = SAlloc(PERS_END)
    x_sl = [Sh.t(f"x_sl{i}", [128, EC, SLAB], F32) for i in range(2)]
    win_f = [Sh.t(f"win_f{i}", [128, NP, DH], F32) for i in range(2)]
    wo1_f = [Sh.t(f"wo1_f{i}", [128, NP, DH], F32) for i in range(2)]
    wo2_f = [Sh.t(f"wo2_f{i}", [128, NP, DH], F32) for i in range(2)]
    win_b = Sh.t("win_b", [128, NP, DH], BF16)
    wo1_b = Sh.t("wo1_b", [128, NP, DH], BF16)
    wo2_b = Sh.t("wo2_b", [128, NP, DH], BF16)
    wdec_s = Sh.t("wdec_s", [128, NP, DH], F32)
    gs_s = Sh.t("gs_s", [128, NP, DH], F32)
    xsq_b = [Sh.t(f"xsq_b{i}", [128, SLAB], BF16) for i in range(2)]
    xb2_b = [Sh.t(f"xb2_b{i}", [128, SLAB], BF16) for i in range(2)]
    SH_END = Sh.off

    A = SAlloc(SH_END)
    wo_sb = A.t("wo_sb", [128, EC, DM], BF16)
    qk_sl = [A.t(f"qk_sl{i}", [128, NP, 2, SLAB], BF16) for i in range(2)]
    v_sl = [A.t(f"v_sl{i}", [128, NP, SLAB], BF16) for i in range(2)]
    kt_sl = A.t("kt_sl", [128, SLAB // 128, NP, 128], BF16)
    vt_sl = A.t("vt_sl", [128, SLAB // 128, NP, 128], BF16)
    o_sl = [A.t(f"o_sl{i}", [128, NP, SLAB], BF16) for i in range(2)]
    x2_sl = A.t("x2_sl", [128, EC, SLAB], F32)
    e_sb = A.t("e_sb", [128, 2, 512], BF16)
    p_sb = A.t("p_sb", [128, 2, 512], BF16)
    sums_sb = A.t("sums_sb", [128, 16], F32)
    rcp_sb = A.t("rcp_sb", [128, 16], F32)
    pt_sb = A.t("pt_sb", [128, 4, 128], BF16)
    ktr = [A.t(f"ktr{i}", [128, NP, 128], BF16) for i in range(2)]
    vtr = [A.t(f"vtr{i}", [128, NP, 128], BF16) for i in range(2)]
    pa_sb = A.t("pa_sb", [128, 4, 2, DH], F32)
    s_sb = A.t("s_sb", [128, 8], F32)
    u_sb = A.t("u_sb", [128, 4, 2, DH], BF16)
    gz_sb = A.t("gz_sb", [128, 4, 128], BF16)
    maps_sb = A.t("maps_sb", [128, 4, NP, NCHS], F32)
    scat_sb = A.t("scat_sb", [64, NP, NCHS], F32)
    wdec1_s = A.t("wdec1_s", [128, NP, DH], F32)
    wdec2_s = A.t("wdec2_s", [128, NP, DH], F32)
    gs_s1 = A.t("gs_s1", [128, NP, DH], F32)
    gs_s2 = A.t("gs_s2", [128, NP, DH], F32)
    aTs_sb = A.t("aTs_sb", [128, 2, 256], F32)
    A_END = A.off
    assert A_END <= ARENA_TOP, f"P2 SBUF over: {A_END} > {ARENA_TOP}"

    def off_of(h):
        return nc.lookup_mloc(h).addr

    B = SAlloc(SH_END)
    wq_sb = B.t("wq_sb", [128, EC, DM], BF16)
    wk_sb = B.t("wk_sb", [128, EC, DM], BF16)
    wv_sb = B.t("wv_sb", [128, EC, DM], BF16)
    wlb_sb = B.t("wlb_sb", [128, EC, 64], BF16)
    xln_sl2 = [B.t(f"xln_sl{i}", [128, EC, SLAB], BF16) for i in range(2)]
    qst_b = [B.t(f"qst_b{i}", [128, 512], BF16) for i in range(8)]
    wstage = [B.t(f"wstage{i}", [128, FF], F32) for i in range(2)]
    wtern = B.t("wtern", [128, FF], BF16)
    wtern2 = B.t("wtern2", [128, FF], BF16)
    t2buf = B.t("t2buf", [128, FF], BF16)
    rb_sb = B.t("rb_sb", [128, SLAB], F32)
    mrb_sb = B.t("mrb_sb", [128, SLAB], F32)
    tmp_sl = B.t("tmp_sl", [128, SLAB], F32)
    B_END = B.off
    assert B_END <= ARENA_TOP, f"P1 SBUF over: {B_END} > {ARENA_TOP}"

    C = SAlloc(PERS_END)
    w1t_sb = C.t("w1t_sb", [128, EC, FF], BF16)
    w2t_sb = C.t("w2t_sb", [128, NFC, DM], BF16)
    x2c = [C.t(f"x2c{i}", [128, EC, CSLAB], F32) for i in range(2)]
    x2ln_c = [C.t(f"x2ln_c{i}", [128, EC, CSLAB], BF16) for i in range(2)]
    outc = [C.t(f"outc{i}", [128, EC, CSLAB], F32) for i in range(2)]
    h_sb = [C.t(f"h_sb{i}", [128, CSLAB], BF16) for i in range(2)]
    rows2_sb = C.t("rows2_sb", [8, 128], BF16)
    repr_c = [C.t(f"repr_c{i}", [128, CSLAB], F32) for i in range(2)]
    repmr_c = [C.t(f"repmr_c{i}", [128, CSLAB], F32) for i in range(2)]
    tmp_c = C.t("tmp_c", [128, CSLAB], F32)
    C_END = C.off
    assert C_END <= ARENA_TOP, f"P3 SBUF over: {C_END} > {ARENA_TOP}"

    nc.alloc_sbuf_tensor("resv", [128, ARENA_TOP - nc.sbuf_base], mybir.dt.uint8)

    _pb = {}
    for i in (0, 1, 2):
        _pb[i] = nc.alloc_psum_tensor(f"psB{i}", [128, 512], F32)
    b34 = nc.alloc_psum_tensor("psB34", [128, 1024], F32)
    for i in (5, 6, 7):
        _pb[i] = nc.alloc_psum_tensor(f"psB{i}", [128, 512], F32)

    def PB(i):
        if i == 3:
            return b34[:, 0:512]
        if i == 4:
            return b34[:, 512:1024]
        return _pb[i][:, :]

    def sem(name, step=1):
        h = nc.ctx.enter_context(nc.semaphore(name))
        return SemC(h, step)

    dW = sem("dW", 16)
    dX = sem("dX", 16)
    dXp = [sem("dXp0", 16), sem("dXp1", 16)]
    dQp = [sem("dQp0", 16), sem("dQp1", 16)]
    dTlp = [sem("dTlp0", 16), sem("dTlp1", 16)]
    dCx = [sem("dCx0", 16), sem("dCx1", 16)]
    dQ = sem("dQ", 16)
    dR = sem("dR", 16)
    dX2 = sem("dX2", 16)
    dTl = sem("dTl", 16)
    dTw = sem("dTw", 16)
    dTw2 = sem("dTw2", 16)
    dC = sem("dC", 16)
    dY = sem("dY", 16)
    tT = sem("tT")
    tV = sem("tV")
    tS = sem("tS")
    tG = sem("tG")

    E = Emit(nc)

    def _ckpt(tag):
        if trunc is not None and trunc == tag:
            E.frozen = True

    S = {
        "cur": 0, "initcast": 0, "cast1": 0, "cast2": 0, "kvt": {}, "maps_v": {},
        "b5_free": 0, "b6_free": 0, "b7V": 0, "b7S": 0,
        "tern_g": 0, "tern_v": 0, "kvt_ev": 0,
        "exp_k": {}, "exp_q": {}, "oevac": {}, "ptevac": {}, "aTev": {},
        "pk": {}, "pq": {}, "pa_v": {}, "u": {}, "gz": {},
        "gs1": {}, "gs2": {}, "gs3": {}, "mm1k": {}, "mm1q": {}, "aT": {}, "o": {},
        "pT": {}, "G1": {}, "G2": {}, "Gin": {}, "rep": {}, "xev": {},
        "repz": 0, "pw": {}, "u1f": {}, "u2f": {}, "u3f": {}, "u3b": {},
        "womm": 0,
    }

    def gdma(out, in_, s=dW):
        return E.op("gpsimd", lambda E_, o=out, i=in_: E_.dma_start(out=o, in_=i), inc=s)

    gdma(win_f[0][:, :, :], wini_d[:, :].rearrange("p (a b) -> p a b", b=DH))
    gdma(wo1_f[0][:, :, :], wo1i_d[:, :].rearrange("p (a b) -> p a b", b=DH))
    gdma(wo2_f[0][:, :, :], wo2i_d[:, :].rearrange("p (a b) -> p a b", b=DH))
    gdma(ident_b[:, :], idn_d[:, :])
    gdma(selm_sb[:, :, :], selm_d[:, :].rearrange("p (a b) -> p a b", b=128))
    gdma(maskp_sb[:, :], maskp_d[:, :])
    gdma(ones1_sb[:, :], ones1_d[:, :])
    gdma(onesc_sb[:, :], onesc_d[:, :])
    gdma(onesb_sb[:, :], onesc_d[:, :])
    gdma(b1_sb[:, :], b1_d[:, :])
    gdma(b2_sb[:, :], b2_d[:, :])
    gdma(cst_sb[:, :], cst_d[:, :])
    gdma(pick_sb[:, :, :], pick_d[:, :].rearrange("p (a b) -> p a b", b=128))
    W_CONST = dW.n

    gdma(wq_sb[:, :, :], wq_d[:, :, :].rearrange("e p m -> p e m"))
    gdma(wk_sb[:, :, :], wk_d[:, :, :].rearrange("e p m -> p e m"))
    gdma(wv_sb[:, :, :], wv_d[:, :, :].rearrange("e p m -> p e m"))
    gdma(wlb_sb[:, :, :], wlb_d[:, :, :].rearrange("e p m -> p e m"))
    W_LOADS = dW.n

    E.op("sync", lambda E_: E_.dma_start(out=x_sl[0][:, :, :], in_=x_d[0, :, :, :]), inc=dXp[0])
    XP1 = {0: 16}

    E.op("scalar", lambda E_: _table_load(E_, set_exp, nc))
    E.wait("scalar", dW, W_LOADS)
    E.op("scalar", lambda E_: E_.copy(out=win_b[:, :, :], in_=win_f[0][:, :, :]))
    E.op("scalar", lambda E_: E_.copy(out=wo1_b[:, :, :], in_=wo1_f[0][:, :, :]))
    S["initcast"] = S["cast1"] = S["cast2"] = E.op("scalar", lambda E_: E_.copy(
        out=wo2_b[:, :, :], in_=wo2_f[0][:, :, :]), inc=tS)

    E.op("vector", lambda E_: E_.memset(b34[:, :], 0.0))
    if dump:
        for _b in (o_sl[0], o_sl[1], p_sb, pt_sb, gz_sb, e_sb, s_sb, pa_sb,
                   u_sb, sums_sb, rcp_sb, x2_sl, maps_sb):
            E.op("vector", lambda E_, b=_b: E_.memset(
                b if len(b.shape) == 2 else b, 0.0))

    def emit_ln_stats(xb_of, cast_of, store_t, store_s):
        sum_mm = {}
        sq_mm = {}
        sq_v = {}
        E.wait("tensor", tV, S["b5_free"])
        E.wait("tensor", tV, S["b6_free"])
        for ec in range(EC):
            cv = cast_of(ec, sum_mm.get(ec - 2, 0), sq_v.get(ec - 2, 0))
            E.wait("tensor", tS, cv)
            E.wait("vector", tS, cv)
            xb = xb_of(ec)
            for g in range(NG):
                last = E.op("tensor", lambda E_, xb=xb, g=g, ec=ec: E_.matmul(
                    PB(6)[:, ec * NG + g:ec * NG + g + 1],
                    xb[:, 128 * g:128 * (g + 1)], onesb_sb[:, :],
                    start=True, stop=True),
                    inc=(tT if g == NG - 1 else None))
            sum_mm[ec] = last
            if ec >= 2:
                E.wait("vector", tT, sq_mm[ec - 2])
            sq_v[ec] = E.op("vector", lambda E_, xb=xb, b=xsq_b[ec % 2]: E_.tensor_tensor(
                out=b[:, :], in0=xb, in1=xb, op=ALU.mult), inc=tV)
            E.wait("tensor", tV, sq_v[ec])
            for g in range(NG):
                last = E.op("tensor", lambda E_, b=xsq_b[ec % 2], g=g, ec=ec: E_.matmul(
                    PB(5)[:, ec * NG + g:ec * NG + g + 1],
                    b[:, 128 * g:128 * (g + 1)], onesb_sb[:, :],
                    start=True, stop=True),
                    inc=(tT if g == NG - 1 else None))
            sq_mm[ec] = last
        E.wait("vector", tT, sq_mm[EC - 1])
        E.op("vector", lambda E_: E_.reduce_sum(
            out=st_sb[:, 0:NG],
            in_=PB(6)[:, 0:NG * EC].rearrange("p (e g) -> p g e", g=NG),
            axis=AX.X))
        vq = E.op("vector", lambda E_: E_.reduce_sum(
            out=st_sb[:, 4:4 + NG],
            in_=PB(5)[:, 0:NG * EC].rearrange("p (e g) -> p g e", g=NG),
            axis=AX.X), inc=tV)
        E.wait("vector", tV, vq)
        S["b5_free"] = vq
        S["b6_free"] = vq
        vv = E.op("vector", lambda E_: E_.tensor_scalar(
            out=st_sb[:, 8:8 + NG], in0=st_sb[:, 0:NG], scalar1=1.0 / DM,
            scalar2=None, op0=ALU.mult), inc=tV)
        E.wait("vector", tV, vv)
        vv = E.op("vector", lambda E_: E_.tensor_tensor(
            out=st_sb[:, 12:12 + NG], in0=st_sb[:, 8:8 + NG],
            in1=st_sb[:, 8:8 + NG], op=ALU.mult), inc=tV)
        E.wait("vector", tV, vv)
        vvar = E.op("vector", lambda E_: E_.scalar_tensor_tensor(
            out=st_sb[:, 0:NG], in0=st_sb[:, 4:4 + NG], scalar=1.0 / DM,
            in1=st_sb[:, 12:12 + NG], op0=ALU.mult, op1=ALU.subtract), inc=tV)
        E.wait("scalar", tV, vvar)
        sv = E.op("scalar", lambda E_: E_.activation(
            out=st_sb[:, 12:12 + NG], in_=st_sb[:, 0:NG], func=ACT.Ln, bias=EPS), inc=tS)
        E.wait("scalar", tS, sv)
        srt = E.op("scalar", lambda E_: E_.activation(
            out=st_sb[:, 4:4 + NG], in_=st_sb[:, 12:12 + NG], func=ACT.Exp, scale=-0.5), inc=tS)
        E.wait("vector", tS, srt)
        vmr = E.op("vector", lambda E_: E_.tensor_tensor(
            out=st_sb[:, 0:NG], in0=st_sb[:, 8:8 + NG], in1=st_sb[:, 4:4 + NG],
            op=ALU.mult), inc=tV)
        E.wait("vector", tV, vmr)
        vst = E.op("vector", lambda E_, t=store_t, s=store_s: E_.tensor_copy(
            out=t[:, s, :], in_=st_sb[:, 0:8]), inc=tV)
        E.wait("vector", tV, vst)
        return vst

    def emit_bcast(stats_t, s, wait_T_extra=0):
        vc8 = E.op("vector", lambda E_, t=stats_t, s=s: E_.tensor_copy(
            out=st8b[:, :], in_=t[:, s, :]), inc=tV)
        E.wait("tensor", tV, vc8)
        E.wait("tensor", tV, S["b6_free"])
        if wait_T_extra:
            E.wait("tensor", tT, wait_T_extra)
        ttr = E.op("tensor", lambda E_: E_.transpose(
            PB(6)[:, :].bitcast(BF16)[0:8, 0:128], st8b[:, :], ident_b[:, :]), inc=tT)
        E.wait("vector", tT, ttr)
        vrw = E.op("vector", lambda E_: E_.tensor_copy(
            out=rows_sb[:, :], in_=PB(6)[:, :].bitcast(BF16)[0:8, 0:128]), inc=tV)
        S["b6_free"] = vrw
        E.wait("tensor", tV, vrw)
        E.wait("tensor", tV, S["b7V"])
        E.wait("tensor", tS, S["b7S"])
        for g in range(NG):
            last = E.op("tensor", lambda E_, g=g: E_.matmul(
                PB(7)[:, 128 * g:128 * (g + 1)], pick_sb[:, g, :], rows_sb[:, :],
                start=True, stop=True), inc=(tT if g == NG - 1 else None))
        E.wait("vector", tT, last)
        vm = E.op("vector", lambda E_: E_.tensor_copy(out=mrb_sb[:, :], in_=PB(7)[:, :]), inc=tV)
        E.wait("tensor", tV, vm)
        for g in range(NG):
            last = E.op("tensor", lambda E_, g=g: E_.matmul(
                PB(7)[:, 128 * g:128 * (g + 1)], pick_sb[:, 4 + g, :], rows_sb[:, :],
                start=True, stop=True), inc=(tT if g == NG - 1 else None))
        E.wait("vector", tT, last)
        vrb = E.op("vector", lambda E_: E_.tensor_copy(out=rb_sb[:, :], in_=PB(7)[:, :]), inc=tV)
        S["b7V"] = vrb
        return vrb

    TIDX = EC + NFC

    def tern_src(idx):
        if idx < EC:
            return w1_d[idx, :, :], w1ts_d[idx, :, :], FF, 0
        return w2_d[idx - EC, :, :], w2ts_d[idx - EC, :, :], DM, 2

    tern_state = {"last_g": {}, "thr_v": 0, "wr": {0: 0, 1: 0}}
    TORDER = []
    _w2i = iter(range(EC, TIDX))
    for _k in range(EC):
        TORDER.append(_k)
        for _ in range(4):
            _n2 = next(_w2i, None)
            if _n2 is not None:
                TORDER.append(_n2)

    def tern_scale_unit(pos):
        idx = TORDER[pos]
        src, _, wid, _ = tern_src(idx)
        buf = wstage[pos % 2]
        dl = dTlp[pos % 2]
        eng = "gpsimd" if pos % 2 == 0 else "sync"
        if pos >= 2:
            E.wait(eng, tG, tern_state["last_g"][pos - 2])
        E.op(eng, lambda E_, src=src, w=wid, b=buf: E_.dma_start(
            out=b[:, 0:w], in_=src), inc=dl)
        E.wait("vector", dl, dl.n)
        tern_state["last_g"][pos] = E.op("vector", lambda E_, w=wid, b=buf, idx=idx: E_.tensor_reduce(
            out=wtpart[:, idx:idx + 1], in_=b[:, 0:w], axis=AX.X,
            op=ALU.add, apply_absolute_value=True), inc=tG)

    def tern_thresholds():
        E.wait("vector", tG, tern_state["last_g"][TIDX - 1])
        va = E.op("vector", lambda E_: E_.reduce_sum(
            out=wtot[:, 0:1], in_=wtpart[:, 0:EC], axis=AX.X), inc=tV)
        E.wait("vector", tV, va)
        vb = E.op("vector", lambda E_: E_.reduce_sum(
            out=wtot[:, 1:2], in_=wtpart[:, EC:EC + NFC], axis=AX.X), inc=tV)
        E.wait("vector", tV, vb)
        E.wait("tensor", tV, vb)
        E.wait("tensor", tV, S["b6_free"])
        tt = E.op("tensor", lambda E_: E_.matmul(
            PB(6)[0:2, 0:1], wtot[:, 0:2], onesc_sb[:, :], start=True, stop=True), inc=tT)
        E.wait("vector", tT, tt)
        vv = E.op("vector", lambda E_: E_.tensor_scalar(
            out=PB(6)[0:2, 1:2], in0=PB(6)[0:2, 0:1], scalar1=1.0 / (FF * DM),
            scalar2=1e-8, op0=ALU.mult, op1=ALU.max), inc=tV)
        E.wait("vector", tV, vv)
        vv = E.op("vector", lambda E_: E_.tensor_scalar(
            out=PB(6)[0:2, 2:3], in0=PB(6)[0:2, 1:2], scalar1=0.5,
            scalar2=None, op0=ALU.mult), inc=tV)
        E.wait("vector", tV, vv)
        vv = E.op("vector", lambda E_: E_.tensor_scalar(
            out=PB(6)[0:2, 3:4], in0=PB(6)[0:2, 2:3], scalar1=-1.0,
            scalar2=None, op0=ALU.mult), inc=tV)
        E.wait("vector", tV, vv)
        v4 = E.op("vector", lambda E_: E_.tensor_copy(
            out=wtot[0:2, 0:2], in_=PB(6)[0:2, 2:4]), inc=tV)
        S["b6_free"] = v4
        E.wait("vector", tV, v4)
        E.wait("gpsimd", tV, v4)
        E.op("gpsimd", lambda E_: E_.dma_start(out=trow[0:1, 0:2], in_=wtot[0:1, 0:2]), inc=dTw)
        E.op("gpsimd", lambda E_: E_.dma_start(out=trow[0:1, 2:4], in_=wtot[1:2, 0:2]), inc=dTw)
        E.wait("tensor", dTw, dTw.n)
        E.wait("tensor", tV, S["b7V"])
        E.wait("tensor", tS, S["b7S"])
        tb = E.op("tensor", lambda E_: E_.matmul(
            PB(7)[:, 0:4], ones1_sb[:, :], trow[0:1, 0:4], start=True, stop=True), inc=tT)
        E.wait("vector", tT, tb)
        vtc = E.op("vector", lambda E_: E_.tensor_copy(out=thr_col[:, :], in_=PB(7)[:, 0:4]), inc=tV)
        S["b7V"] = vtc
        E.wait("vector", tV, vtc)
        tern_state["thr_v"] = vtc

    def tern_pass_unit(pos):
        idx = TORDER[pos]
        src, dst, wid, tc = tern_src(idx)
        par = pos % 2
        buf = wstage[par]
        dl = dTlp[par]
        eng = "gpsimd" if par == 0 else "sync"
        res = wtern if par == 0 else wtern2
        dw = dTw if par == 0 else dTw2
        E.wait(eng, tG, tern_state["last_g"][TIDX + pos - 2 if pos >= 2
                                             else TIDX - 2 + pos])
        E.op(eng, lambda E_, src=src, w=wid, b=buf: E_.dma_start(
            out=b[:, 0:w], in_=src), inc=dl)
        E.wait("vector", dl, dl.n)
        E.wait("vector", tV, tern_state["thr_v"])
        g1 = E.op("vector", lambda E_, w=wid, tc=tc, b=buf: E_.tensor_single_scalar(
            out=t2buf[:, 0:w], in_=b[:, 0:w],
            scalar=thr_col[:, tc + 1:tc + 2], op=ALU.is_lt), inc=tG)
        E.wait("vector", tG, g1)
        E.wait("vector", dw, tern_state["wr"][par])
        g2 = E.op("vector", lambda E_, w=wid, tc=tc, b=buf, res=res: E_.scalar_tensor_tensor(
            out=res[:, 0:w], in0=b[:, 0:w], scalar=thr_col[:, tc:tc + 1],
            in1=t2buf[:, 0:w], op0=ALU.is_gt, op1=ALU.subtract), inc=tG)
        tern_state["last_g"][TIDX + pos] = g2
        E.wait(eng, tG, g2)
        E.op(eng, lambda E_, dst=dst, w=wid, res=res: E_.dma_start(
            out=dst, in_=res[:, 0:w]), inc=dw)
        tern_state["wr"][par] = dw.n

    tern_iter = iter(range(TIDX))
    tern_iter2 = iter(range(TIDX))
    qkv_written = {}
    stats_cast_done = {}

    def emit_stats(s):
        E.wait("scalar", dXp[s % 2], XP1[s])

        def cast1(ec, t_free, v_free, s=s):
            E.wait("scalar", tT, t_free)
            E.wait("scalar", tV, v_free)
            v = E.op("scalar", lambda E_, ec=ec, s=s: E_.copy(
                out=xb2_b[ec % 2][:, :], in_=x_sl[s % 2][:, ec, :]), inc=tS)
            stats_cast_done[s] = v
            return v
        emit_ln_stats(lambda ec: xb2_b[ec % 2][:, :], cast1, stats1_sb, s)

    def emit_bcast_xln(s):
        vrb = emit_bcast(stats1_sb, s)
        E.wait("vector", dXp[s % 2], XP1[s])
        E.wait("vector", tV, vrb)
        xln_sl = xln_sl2[s % 2]
        for ec in range(EC):
            E.op("vector", lambda E_, ec=ec, s=s: E_.tensor_tensor(
                out=tmp_sl[:, :], in0=x_sl[s % 2][:, ec, :], in1=rb_sb[:, :], op=ALU.mult))
            last = E.op("vector", lambda E_, ec=ec, xln_sl=xln_sl: E_.tensor_tensor(
                out=xln_sl[:, ec, :], in0=tmp_sl[:, :], in1=mrb_sb[:, :], op=ALU.subtract),
                inc=(tV if ec == EC - 1 else None))
        S[f"xlnv{s}"] = last

    E.op("sync", lambda E_: E_.dma_start(
        out=x_sl[1][:, :, :], in_=x_d[1, :, :, :]), inc=dXp[1])
    XP1[1] = dXp[1].n
    emit_stats(0)
    emit_bcast_xln(0)
    E.wait("tensor", dW, W_LOADS)
    proj_evac = {}
    proj_un = [0]
    stage_dq = {}
    for s in range(NSLAB):
        xln_sl = xln_sl2[s % 2]
        vxln = S[f"xlnv{s}"]
        if s + 2 < NSLAB:
            E.wait("sync", tV, vxln)
            E.wait("sync", tS, stats_cast_done[s])
            E.op("sync", lambda E_, s=s: E_.dma_start(
                out=x_sl[s % 2][:, :, :], in_=x_d[s + 2, :, :, :]), inc=dXp[s % 2])
            XP1[s + 2] = dXp[s % 2].n
        E.op("scalar", lambda E_: _table_load(E_, set_silu, nc))
        E.wait("tensor", tV, vxln)
        for mi, (mat, wsb) in enumerate((("q", wq_sb), ("k", wk_sb), ("v", wv_sb))):
            for oc in range(EC):
                un = proj_un[0]
                bank = un % 2
                if un >= 2:
                    E.wait("tensor", tS, proj_evac[un - 2])
                for kc in range(EC):
                    last = E.op("tensor", lambda E_, w=wsb, oc=oc, kc=kc, bank=bank, xln_sl=xln_sl: E_.matmul(
                        PB(bank)[:, :], w[:, kc, 128 * oc:128 * (oc + 1)], xln_sl[:, kc, :],
                        start=(kc == 0), stop=(kc == EC - 1)),
                        inc=(tT if kc == EC - 1 else None))
                E.wait("scalar", tT, last)
                stg = qst_b[un % 8]
                if un >= 8:
                    E.wait("scalar", dQ, stage_dq[un - 8])
                if mat == "v":
                    sev = E.op("scalar", lambda E_, st=stg, bank=bank: E_.copy(
                        out=st[:, :], in_=PB(bank)[:, :]), inc=tS)
                else:
                    sev = E.op("scalar", lambda E_, st=stg, bank=bank: E_.activation(
                        out=st[:, :], in_=PB(bank)[:, :], func=ACT.Silu), inc=tS)
                proj_evac[un] = sev
                if mat == "q":
                    dst = qks_d[s, :, oc, 0, :]
                elif mat == "k":
                    dst = qks_d[s, :, oc, 1, :]
                else:
                    dst = vs_d[s, :, oc, :]
                E.op("scalar", lambda E_, d=dst, st=stg: E_.dma_start(out=d, in_=st[:, :]), inc=dQ)
                stage_dq[un] = dQ.n
                proj_un[0] = un + 1
        qkv_written[s] = dQ.n
        E.wait("tensor", tV, S["b7V"])
        E.wait("tensor", tS, S["b7S"])
        for kc in range(EC):
            last = E.op("tensor", lambda E_, kc=kc, xln_sl=xln_sl: E_.matmul(
                PB(7)[0:64, :], wlb_sb[:, kc, :], xln_sl[:, kc, :],
                start=(kc == 0), stop=(kc == EC - 1)), inc=(tT if kc == EC - 1 else None))
        if s == NSLAB - 1:
            S["lastproj_t"] = last
        E.wait("scalar", tT, last)
        stnh = E.op("scalar", lambda E_: E_.activation(
            out=tmp_sl[0:64, :], in_=PB(7)[0:64, :], func=ACT.Tanh, scale=0.5), inc=tS)
        S["b7S"] = stnh
        E.wait("vector", tS, stnh)
        E.op("vector", lambda E_: E_.tensor_scalar(
            out=tmp_sl[64:128, :], in0=tmp_sl[0:64, :], scalar1=0.5, scalar2=0.5,
            op0=ALU.mult, op1=ALU.add))
        vrd = E.op("vector", lambda E_, s=s: E_.reduce_sum(
            out=lrwdc_all[:, s, :], in_=tmp_sl[64:128, :].rearrange("p (a b) -> p a b", b=CHUNK),
            axis=AX.X), inc=tV)
        E.wait("vector", tV, vrd)
        E.op("vector", lambda E_, s=s: E_.tensor_scalar(
            out=lrwdc_all[0:32, s, :], in0=lrwdc_all[0:32, s, :], scalar1=BASE_LR / CHUNK,
            scalar2=None, op0=ALU.mult))
        vv = E.op("vector", lambda E_, s=s: E_.tensor_scalar(
            out=lrwdc_all[32:64, s, :], in0=lrwdc_all[32:64, s, :], scalar1=BASE_WD / CHUNK,
            scalar2=None, op0=ALU.mult), inc=tV)
        S[f"lrwd{s}"] = vv
        if s + 1 < NSLAB:
            E.op("scalar", lambda E_: _table_load(E_, set_exp, nc))
            emit_stats(s + 1)
            emit_bcast_xln(s + 1)
        if s < 4:
            for _ in range(10):
                nxt = next(tern_iter, None)
                if nxt is not None:
                    tern_scale_unit(nxt)
        else:
            if s == 4:
                tern_thresholds()
            for _ in range(10):
                nxt = next(tern_iter2, None)
                if nxt is not None:
                    tern_pass_unit(nxt)
    TERN_W = dTw.n
    TERN_W2 = dTw2.n
    P1_T, P1_V, P1_S = tT.n, tV.n, tS.n
    _ckpt("p1b")

    LASTPROJ = S["lastproj_t"]
    E.wait("gpsimd", tT, LASTPROJ)
    gdma(wo_sb[:, :, :], wo_d[:, :, :].rearrange("e p m -> p e m"))
    WO_L = dW.n
    E.wait("scalar", tT, LASTPROJ)
    for s0 in (0, 1):
        E.wait("scalar", dQ, qkv_written[s0])
        E.op("scalar", lambda E_, s0=s0: E_.dma_start(
            out=qk_sl[s0 % 2][:, :, :, :], in_=qks_d[s0, :, :, :, :]), inc=dQp[s0 % 2])
        E.op("scalar", lambda E_, s0=s0: E_.dma_start(
            out=v_sl[s0 % 2][:, :, :], in_=vs_d[s0, :, :, :]), inc=dQp[s0 % 2])
        S[f"p2ldqk{s0}"] = dQp[s0 % 2].n

    E.op("scalar", lambda E_: _table_load(E_, set_exp, nc))
    P1_G = tG.n
    for eng in ("sync", "gpsimd", "scalar", "vector", "tensor"):
        E.wait(eng, tT, P1_T)
        E.wait(eng, tV, P1_V)
        E.wait(eng, tS, P1_S)
        E.wait(eng, tG, P1_G)
        E.wait(eng, dTw, TERN_W)
        E.wait(eng, dTw2, TERN_W2)
        E.wait(eng, dTlp[0], dTlp[0].n)
        E.wait(eng, dTlp[1], dTlp[1].n)
        E.wait(eng, dXp[0], dXp[0].n)
        E.wait(eng, dXp[1], dXp[1].n)
        E.wait(eng, dQ, dQ.n)
        E.wait(eng, dX, dX.n)
    E.wait("tensor", dW, WO_L)
    for buf in (ktr[0], ktr[1], vtr[0], vtr[1]):
        mz = E.op("vector", lambda E_, b=buf: E_.memset(b[:, :, :], 0.0), inc=tV)
    S["repz"] = mz

    def p2_loads_qk(s):
        if s < 2:
            return
        E.wait("sync", dQ, qkv_written[s])
        if s >= 2:
            E.wait("sync", tT, S["lastuse_qk"][s - 2])
        E.op("sync", lambda E_, s=s: E_.dma_start(
            out=qk_sl[s % 2][:, :, :, :], in_=qks_d[s, :, :, :, :]), inc=dQp[s % 2])
        E.op("sync", lambda E_, s=s: E_.dma_start(
            out=v_sl[s % 2][:, :, :], in_=vs_d[s, :, :, :]), inc=dQp[s % 2])
        S[f"p2ldqk{s}"] = dQp[s % 2].n

    def p2_loads_x(s):
        if s >= 2:
            E.wait("sync", tV, S["xev"].get(s - 2, 0))
        E.op("sync", lambda E_, s=s: E_.dma_start(
            out=x_sl[s % 2][:, :, :], in_=x_d[s, :, :, :]), inc=dXp[s % 2])
        S[f"p2ldx{s}"] = dXp[s % 2].n

    S["lastuse_qk"] = {}

    def p2_maps(s):
        E.wait("vector", tV, S[f"lrwd{s}"])
        if s >= 1:
            E.wait("vector", tV, S["maps_v"][s - 1])
            E.wait("vector", tG, S["pw"][s * NCHS - 1])
        vsc = E.op("vector", lambda E_, s=s: E_.tensor_tensor(
            out=scat_sb[:, :, :],
            in0=maskp_sb[:, :].unsqueeze(2).broadcast_to([64, NP, NCHS]),
            in1=lrwdc_all[:, s, :].unsqueeze(1).broadcast_to([64, NP, NCHS]),
            op=ALU.mult), inc=tV)
        E.wait("tensor", tV, vsc)
        E.wait("tensor", tV, S["b7V"])
        E.wait("tensor", tS, S["b7S"])
        for m in range(4):
            last = E.op("tensor", lambda E_, m=m: E_.matmul(
                PB(7)[:, m * NP * NCHS:(m + 1) * NP * NCHS],
                selm_sb[:, m, :], scat_sb[:, :, :].rearrange("p a b -> p (a b)"),
                start=True, stop=True), inc=(tT if m == 3 else None))
        E.wait("vector", tT, last)
        S["maps_v"][s] = E.op("vector", lambda E_: E_.tensor_copy(
            out=maps_sb[:, :, :, :].rearrange("p a b c -> p (a b c)"),
            in_=PB(7)[:, 0:4 * NP * NCHS]), inc=tV)
        S["b7V"] = S["maps_v"][s]

    def kvt_subunits(s1, j, subs, rep_gate):
        ptH = PB(2)[:, :].bitcast(BF16)
        for (is_k, half) in subs:
            dstt = kt_sl if is_k else vt_sl
            E.wait("tensor", dQp[s1 % 2], S[f"p2ldqk{s1}"])
            E.wait("tensor", dR, rep_gate)
            E.wait("tensor", tS, S["kvt_ev"])
            E.wait("tensor", dTw, TERN_W)
            E.wait("tensor", dTw2, TERN_W2)
            for pq in range(4):
                pair = half * 4 + pq
                if is_k:
                    in_ap = qk_sl[s1 % 2][:, pair, 1, 128 * j:128 * (j + 1)]
                else:
                    in_ap = v_sl[s1 % 2][:, pair, 128 * j:128 * (j + 1)]
                last = E.op("tensor", lambda E_, i_=in_ap, pq=pq: E_.transpose(
                    ptH[:, 512 + 128 * pq:512 + 128 * (pq + 1)], i_, ident_b[:, :]),
                    inc=(tT if pq == 3 else None))
            E.wait("scalar", tT, last)
            S["kvt_ev"] = E.op("scalar", lambda E_, d=dstt, j=j, h=half: E_.copy(
                out=d[:, j, 4 * h:4 * (h + 1), :],
                in_=ptH[:, 512:1024]), inc=tS)
        return S["kvt_ev"]

    def p2_kvt_full(s):
        E.wait("scalar", dR, dR.n)
        for j in range(SLAB // 128):
            S["kvt"][s] = kvt_subunits(
                s, j, [(True, 0), (True, 1), (False, 0), (False, 1)], dR.n)

    def emit_chunk(c, tA=None, tMid=None, tTail=None):
        s = c // NCHS
        i = c % NCHS
        cp = i // 2
        half = (i % 2) * 64
        cur = S["cur"]
        nxt = 1 - cur
        rb = c % 2
        sb = s % 2
        ptB2 = PB(2)[:, :].bitcast(BF16)
        b0v = PB(0)[:, :].rearrange("p (q n) -> p q n", n=DH)
        b1v = PB(1)[:, :].rearrange("p (q n) -> p q n", n=DH)
        b6v = PB(6)[:, :].rearrange("p (q n) -> p q n", n=DH)
        aTv = aTs_sb[:, :, :].rearrange("p g (j n) -> p j g n", n=64)
        ptPv = ptB2[:, 0:512].rearrange("p (j g n) -> p j g n", j=4, n=64)
        ptSv = pt_sb[:, :, :].rearrange("p j (g n) -> p j g n", n=64)

        def mapsl(m, i):
            return maps_sb[:, m, :, i].unsqueeze(2).broadcast_to([128, NP, DH])

        E.wait("gpsimd", tV, S["maps_v"][s])
        if c == 0:
            E.wait("gpsimd", dW, W_CONST)
        if c >= 1:
            E.wait("gpsimd", tV, S["u3b"][c - 1])
        E.op("gpsimd", lambda E_, i=i, cur=cur: E_.tensor_tensor(
            out=wdec_s[:, :, :], in0=win_f[cur][:, :, :], in1=mapsl(2, i), op=ALU.mult))
        E.op("gpsimd", lambda E_, i=i, cur=cur: E_.tensor_tensor(
            out=wdec1_s[:, :, :], in0=wo1_f[cur][:, :, :], in1=mapsl(3, i), op=ALU.mult))
        S["pw"][c] = E.op("gpsimd", lambda E_, i=i, cur=cur: E_.tensor_tensor(
            out=wdec2_s[:, :, :], in0=wo2_f[cur][:, :, :], in1=mapsl(3, i), op=ALU.mult), inc=tG)

        E.wait("scalar", tS, S["kvt"][s])
        if c < 2:
            E.wait("scalar", tV, S["repz"])
        if c >= 2:
            E.wait("scalar", tT, S["Gin"][c - 2])
        for (src, dst) in ((kt_sl, ktr[rb]), (vt_sl, vtr[rb])):
            for par in range(2):
                E.op("scalar", lambda E_, s_=src, d=dst, cp=cp, hh=half, par=par: E_.dma_start(
                    out=d[64 * par:64 * (par + 1), :, :].rearrange(
                        "p (a b) c -> p a b c", b=2)[:, :, par, :],
                    in_=s_[hh:hh + 64, cp, :, :].rearrange(
                        "p (a b) c -> p a b c", b=2)[:, :, par, :]), inc=dR)
        S["rep"][c] = dR.n
        _ckpt(f"c{c}:rep")

        if c == 0:
            E.wait("tensor", tS, S["initcast"])
            E.wait("tensor", tV, S["repz"])
        else:
            E.wait("tensor", tV, S["u3b"][c - 1])
            E.wait("tensor", tS, S["exp_k"][c - 1])
        if i == 0:
            E.wait("tensor", dQp[s % 2], S[f"p2ldqk{s}"])
        for pair in range(NP):
            for ch in range(2):
                last = E.op("tensor", lambda E_, p=pair, ch=ch, i=i, sb=sb: E_.matmul(
                    PB(1)[64 * ch:64 * (ch + 1), 64 * p:64 * (p + 1)],
                    win_b[64 * ch:64 * (ch + 1), p, :],
                    qk_sl[sb][64 * ch:64 * (ch + 1), p, 1, CHUNK * i:CHUNK * (i + 1)],
                    start=True, stop=True, tile_position=(64 * ch, 64 * ch)),
                    inc=(tT if (pair == NP - 1 and ch == 1) else None))
        S["mm1k"][c] = last
        if c >= 1:
            E.wait("tensor", tS, S["exp_q"][c - 1])
        for pair in range(NP):
            for ch in range(2):
                last = E.op("tensor", lambda E_, p=pair, ch=ch, i=i, sb=sb: E_.matmul(
                    PB(0)[64 * ch:64 * (ch + 1), 64 * p:64 * (p + 1)],
                    win_b[64 * ch:64 * (ch + 1), p, :],
                    qk_sl[sb][64 * ch:64 * (ch + 1), p, 0, CHUNK * i:CHUNK * (i + 1)],
                    start=True, stop=True, tile_position=(64 * ch, 64 * ch)),
                    inc=(tT if (pair == NP - 1 and ch == 1) else None))
        S["mm1q"][c] = last
        _ckpt(f"c{c}:mm1")

        E.wait("tensor", tS, S["cast2"])
        if c == 0:
            E.wait("tensor", tV, S["repz"])
        if c >= 1:
            E.wait("tensor", tS, S["aTev"][c - 1])
        for pair in range(NP):
            pp = pair % 2
            j = pair // 2
            for ch in range(2):
                last = E.op("tensor", lambda E_, p=pair, ch=ch, pp=pp, j=j, i=i, sb=sb: E_.matmul(
                    PB(3 + ch)[64 * pp:64 * (pp + 1), 64 * j:64 * (j + 1)],
                    v_sl[sb][64 * ch:64 * (ch + 1), p, CHUNK * i:CHUNK * (i + 1)],
                    wo2_b[64 * ch:64 * (ch + 1), p, :],
                    start=True, stop=True, tile_position=(64 * ch, 64 * pp)),
                    inc=(tT if (pair == NP - 1 and ch == 1) else None))
        S["aT"][c] = last
        _ckpt(f"c{c}:aT")

        if tA is not None:
            tA()

        E.wait("scalar", tT, S["mm1k"][c])
        S["exp_k"][c] = E.op("scalar", lambda E_: E_.activation(
            out=e_sb[:, 1, :], in_=PB(1)[:, :], func=ACT.Exp), inc=tS)
        E.wait("scalar", tT, S["mm1q"][c])
        S["exp_q"][c] = E.op("scalar", lambda E_: E_.activation(
            out=e_sb[:, 0, :], in_=PB(0)[:, :], func=ACT.Exp), inc=tS)
        E.wait("scalar", tT, S["aT"][c])
        if c >= 1:
            E.wait("scalar", tV, S["u"][c - 1])
        E.op("scalar", lambda E_: E_.copy(
            out=aTs_sb[:, 0, :], in_=b34[:, 0:256]))
        S["aTev"][c] = E.op("scalar", lambda E_: E_.copy(
            out=aTs_sb[:, 1, :], in_=b34[:, 512:768]), inc=tS)

        E.wait("vector", tS, S["exp_k"][c])
        v1 = E.op("vector", lambda E_: E_.reduce_sum(
            out=sums_sb[:, 8:16], in_=e_sb[:, 1, :].rearrange("p (q n) -> p q n", n=64),
            axis=AX.X), inc=tV)
        E.wait("vector", tV, v1)
        v2 = E.op("vector", lambda E_: E_.reciprocal(
            out=rcp_sb[:, 8:16], in_=sums_sb[:, 8:16]), inc=tV)
        E.wait("vector", tV, v2)
        S["pk"][c] = E.op("vector", lambda E_: E_.tensor_tensor(
            out=p_sb[:, 1, :].rearrange("p (q n) -> p q n", n=64),
            in0=e_sb[:, 1, :].rearrange("p (q n) -> p q n", n=64),
            in1=rcp_sb[:, 8:16].unsqueeze(2).broadcast_to([128, NP, 64]),
            op=ALU.mult), inc=tV)
        E.wait("vector", tS, S["exp_q"][c])
        v1 = E.op("vector", lambda E_: E_.reduce_sum(
            out=sums_sb[:, 0:8], in_=e_sb[:, 0, :].rearrange("p (q n) -> p q n", n=64),
            axis=AX.X), inc=tV)
        E.wait("vector", tV, v1)
        v2 = E.op("vector", lambda E_: E_.reciprocal(
            out=rcp_sb[:, 0:8], in_=sums_sb[:, 0:8]), inc=tV)
        E.wait("vector", tV, v2)
        S["pq"][c] = E.op("vector", lambda E_: E_.tensor_tensor(
            out=p_sb[:, 0, :].rearrange("p (q n) -> p q n", n=64),
            in0=e_sb[:, 0, :].rearrange("p (q n) -> p q n", n=64),
            in1=rcp_sb[:, 0:8].unsqueeze(2).broadcast_to([128, NP, 64]),
            op=ALU.mult), inc=tV)
        _ckpt(f"c{c}:soft")

        E.wait("tensor", tV, S["pk"][c])
        if c >= 1:
            E.wait("tensor", tS, S["ptevac"][c - 1])
        for j in range(4):
            in_ap = p_sb[:, 1, 128 * j:128 * (j + 1)]
            last = E.op("tensor", lambda E_, i_=in_ap, j=j: E_.transpose(
                ptB2[:, 128 * j:128 * (j + 1)], i_, ident_b[:, :]),
                inc=(tT if j == 3 else None))
        S["pT"][c] = last
        _ckpt(f"c{c}:pT")

        E.wait("scalar", tT, S["pT"][c])
        S["ptevac"][c] = E.op("scalar", lambda E_: E_.copy(
            out=pt_sb[:, :, :].rearrange("p a b -> p (a b)"), in_=ptB2[:, 0:512]), inc=tS)

        E.wait("tensor", tS, S["ptevac"][c])
        E.wait("tensor", dR, S["rep"][c])
        if c >= 1:
            E.wait("tensor", tV, S["gs1"][c - 1])
        for pair in range(NP):
            j = pair // 2
            for ch in range(2):
                last = E.op("tensor", lambda E_, p=pair, j=j, ch=ch, rb=rb: E_.matmul(
                    b0v[64 * ch:64 * (ch + 1), p, :],
                    pt_sb[:, j, 64 * ch:64 * (ch + 1)],
                    vtr[rb][:, p, 64 * ch:64 * (ch + 1)],
                    start=True, stop=True, tile_position=(0, 64 * ch)),
                    inc=(tT if (pair == NP - 1 and ch == 1) else None))
        S["G1"][c] = last
        if c >= 1:
            E.wait("tensor", tV, S["gs2"][c - 1])
        for pair in range(NP):
            j = pair // 2
            for ch in range(2):
                last = E.op("tensor", lambda E_, p=pair, j=j, ch=ch, rb=rb: E_.matmul(
                    b1v[64 * ch:64 * (ch + 1), p, :],
                    vtr[rb][:, p, 64 * ch:64 * (ch + 1)],
                    pt_sb[:, j, 64 * ch:64 * (ch + 1)],
                    start=True, stop=True, tile_position=(0, 64 * ch)),
                    inc=(tT if (pair == NP - 1 and ch == 1) else None))
        S["G2"][c] = last
        _ckpt(f"c{c}:G12")

        if tMid is not None:
            tMid()

        E.wait("vector", tS, S["ptevac"][c])
        E.wait("vector", tS, S["aTev"][c])
        S["pa_v"][c] = E.op("vector", lambda E_: E_.tensor_tensor(
            out=pa_sb[:, :, :, :], in0=ptSv, in1=aTv, op=ALU.mult), inc=tV)
        v3 = E.op("vector", lambda E_: E_.reduce_sum(
            out=s_sb[:, :], in_=pa_sb[:, :, :, :].rearrange("p j g n -> p (j g) n"),
            axis=AX.X), inc=tV)
        E.wait("vector", tV, v3)
        S["u"][c] = E.op("vector", lambda E_: E_.tensor_tensor(
            out=u_sb[:, :, :, :],
            in0=aTv,
            in1=s_sb[:, :].rearrange("p (j g) -> p j g", g=2).unsqueeze(3).broadcast_to([128, 4, 2, 64]),
            op=ALU.subtract), inc=tV)
        S["gz"][c] = E.op("vector", lambda E_: E_.tensor_tensor(
            out=gz_sb[:, :, :].rearrange("p j (g n) -> p j g n", n=64),
            in0=ptSv, in1=u_sb[:, :, :, :], op=ALU.mult), inc=tV)
        _ckpt(f"c{c}:gz")

        E.wait("tensor", tV, S["gz"][c])
        if c >= 1:
            E.wait("tensor", tV, max(S["gs3"][c - 1], S["b6_free"]))
        else:
            E.wait("tensor", tV, S["b6_free"])
        for pair in range(NP):
            j = pair // 2
            for ch in range(2):
                last = E.op("tensor", lambda E_, p=pair, j=j, ch=ch, rb=rb: E_.matmul(
                    b6v[64 * ch:64 * (ch + 1), p, :],
                    ktr[rb][:, p, 64 * ch:64 * (ch + 1)],
                    gz_sb[:, j, 64 * ch:64 * (ch + 1)],
                    start=True, stop=True, tile_position=(0, 64 * ch)),
                    inc=(tT if (pair == NP - 1 and ch == 1) else None))
        S["Gin"][c] = last
        _ckpt(f"c{c}:Gin")

        E.wait("tensor", tV, S["pq"][c])
        E.wait("tensor", tS, S["cast1"])
        if c >= 1:
            E.wait("tensor", tS, S["oevac"][c - 1])
        E.wait("tensor", tV, S["b5_free"])
        for pair in range(NP):
            for ch in range(2):
                last = E.op("tensor", lambda E_, p=pair, ch=ch: E_.matmul(
                    PB(5)[64 * ch:64 * (ch + 1), 64 * p:64 * (p + 1)],
                    wo1_b[64 * ch:64 * (ch + 1), p, :],
                    p_sb[64 * ch:64 * (ch + 1), 0, 64 * p:64 * (p + 1)],
                    start=True, stop=True, tile_position=(64 * ch, 64 * ch)),
                    inc=(tT if (pair == NP - 1 and ch == 1) else None))
        S["o"][c] = last
        E.wait("scalar", tT, S["o"][c])
        S["oevac"][c] = E.op("scalar", lambda E_, i=i, sb=sb: E_.copy(
            out=o_sl[sb][:, :, CHUNK * i:CHUNK * (i + 1)],
            in_=PB(5)[:, :]), inc=tS)
        _ckpt(f"c{c}:o")

        if tTail is not None:
            tTail()

        E.wait("vector", tT, S["G1"][c])
        if c >= 1:
            E.wait("vector", tG, S["u1f"][c - 1])
        S["gs1"][c] = E.op("vector", lambda E_, i=i: E_.tensor_tensor(
            out=gs_s1[:, :, :], in0=b0v, in1=mapsl(1, i), op=ALU.mult), inc=tV)
        E.wait("vector", tT, S["G2"][c])
        if c >= 1:
            E.wait("vector", tG, S["u2f"][c - 1])
        S["gs2"][c] = E.op("vector", lambda E_, i=i: E_.tensor_tensor(
            out=gs_s2[:, :, :], in0=b1v, in1=mapsl(1, i), op=ALU.mult), inc=tV)
        E.wait("vector", tT, S["Gin"][c])
        if c >= 1:
            E.wait("vector", tG, S["u3f"][c - 1])
        S["gs3"][c] = E.op("vector", lambda E_, i=i: E_.tensor_tensor(
            out=gs_s[:, :, :], in0=b6v, in1=mapsl(0, i), op=ALU.mult), inc=tV)
        E.wait("vector", tG, S["pw"][c])
        S["u3b"][c] = E.op("vector", lambda E_: E_.tensor_tensor(
            out=win_b[:, :, :], in0=wdec_s[:, :, :], in1=gs_s[:, :, :], op=ALU.add), inc=tV)

        E.wait("gpsimd", tV, S["gs1"][c])
        S["u1f"][c] = E.op("gpsimd", lambda E_, nxt=nxt: E_.tensor_tensor(
            out=wo1_f[nxt][:, :, :], in0=wdec1_s[:, :, :], in1=gs_s1[:, :, :], op=ALU.add), inc=tG)
        E.wait("gpsimd", tV, S["gs2"][c])
        S["u2f"][c] = E.op("gpsimd", lambda E_, nxt=nxt: E_.tensor_tensor(
            out=wo2_f[nxt][:, :, :], in0=wdec2_s[:, :, :], in1=gs_s2[:, :, :], op=ALU.add), inc=tG)
        E.wait("gpsimd", tV, S["gs3"][c])
        S["u3f"][c] = E.op("gpsimd", lambda E_, nxt=nxt: E_.tensor_tensor(
            out=win_f[nxt][:, :, :], in0=wdec_s[:, :, :], in1=gs_s[:, :, :], op=ALU.add), inc=tG)

        E.wait("scalar", tG, S["u1f"][c])
        E.wait("scalar", tT, S["o"][c])
        S["cast1"] = E.op("scalar", lambda E_, nxt=nxt: E_.copy(
            out=wo1_b[:, :, :], in_=wo1_f[nxt][:, :, :]), inc=tS)
        E.wait("scalar", tG, S["u2f"][c])
        S["cast2"] = E.op("scalar", lambda E_, nxt=nxt: E_.copy(
            out=wo2_b[:, :, :], in_=wo2_f[nxt][:, :, :]), inc=tS)
        S["cur"] = nxt
        if i == NCHS - 1:
            S["lastuse_qk"][s] = S["aT"][c]

    def wo_mms(s, ec, first):
        sb = s % 2
        if first and ec == 0:
            E.wait("tensor", tS, S["oevac"][(s + 1) * NCHS - 1])
            E.wait("tensor", dW, WO_L)
        if first:
            E.wait("tensor", tV, S["b7V"])
        ks = range(0, EC // 2) if first else range(EC // 2, EC)
        for k in ks:
            last = E.op("tensor", lambda E_, ec=ec, k=k, sb=sb: E_.matmul(
                PB(7)[:, :], wo_sb[:, k, 128 * ec:128 * (ec + 1)], o_sl[sb][:, k, :],
                start=(k == 0), stop=(k == EC - 1)),
                inc=(tT if k == EC - 1 else None))
        if not first:
            S["womm"] = last

    def wo_xev(s, ec, xev_l):
        sb = s % 2
        if ec == 0:
            E.wait("vector", dXp[s % 2], S[f"p2ldx{s}"])
            if s >= 1:
                E.wait("vector", dX2, 16 * s)
            E.wait("vector", dTw, TERN_W)
            E.wait("vector", dTw2, TERN_W2)
        E.wait("vector", tT, S["womm"])
        xev = E.op("vector", lambda E_, ec=ec, sb=sb: E_.tensor_tensor(
            out=x2_sl[:, ec, :], in0=PB(7)[:, :], in1=x_sl[sb][:, ec, :], op=ALU.add), inc=tV)
        S["b7V"] = xev
        xev_l.append(xev)
        if ec == EC - 1:
            S["xev"][s] = xev

    def p2_stats_store(s, xev_l, last_c):
        S["b6_free"] = max(S["b6_free"], S["gs3"][last_c])
        E.wait("tensor", tS, S["oevac"][last_c])
        E.wait("sync", tV, xev_l[-1])
        E.op("sync", lambda E_, s=s: E_.dma_start(out=x2s_d[s, :, :, :], in_=x2_sl[:, :, :]), inc=dX2)

        def cast2(ec, t_free, v_free):
            E.wait("scalar", tT, t_free)
            E.wait("scalar", tV, max(v_free, xev_l[ec]))
            return E.op("scalar", lambda E_, ec=ec: E_.copy(
                out=xb2_b[ec % 2][:, :], in_=x2_sl[:, ec, :]), inc=tS)
        emit_ln_stats(lambda ec: xb2_b[ec % 2][:, :], cast2, stats2_sb, s)

    p2_loads_qk(0)
    p2_loads_x(0)
    p2_maps(0)
    p2_kvt_full(0)
    for s in range(NSLAB):
        if s + 1 < NSLAB:
            p2_loads_qk(s + 1)
        xev_cur = []
        for i in range(NCHS):
            c = s * NCHS + i
            tA = (lambda i=i: wo_mms(s - 1, i, True)) if s >= 1 else None
            tMid = None
            tTail0 = (lambda i=i: wo_mms(s - 1, i, False)) if s >= 1 else None
            if s + 1 < NSLAB and i % 2 == 1:
                j = i // 2

                def tMid(s=s, j=j):
                    kvt_subunits(s + 1, j, [(True, 0), (True, 1)], S["rep"][s * NCHS + 2 * j + 1])

                def tTail(s=s, j=j, i=i):
                    last = kvt_subunits(s + 1, j, [(False, 0), (False, 1)],
                                        S["rep"][s * NCHS + 2 * j + 1])
                    if j == 3:
                        S["kvt"][s + 1] = last
                    if s >= 1:
                        wo_mms(s - 1, i, False)
            else:
                tTail = tTail0
            emit_chunk(c, tA=tA, tMid=tMid, tTail=tTail)
            if s >= 1:
                wo_xev(s - 1, i, xev_cur)
            _ckpt(f"chunk:{c}")
        if s >= 1:
            p2_stats_store(s - 1, xev_cur, s * NCHS + NCHS - 1)
        if s + 1 < NSLAB:
            p2_maps(s + 1)
            p2_loads_x(s + 1)
        _ckpt(f"slab:{s}")
    xev_cur = []
    for i in range(NCHS):
        wo_mms(NSLAB - 1, i, True)
        wo_mms(NSLAB - 1, i, False)
        wo_xev(NSLAB - 1, i, xev_cur)
    p2_stats_store(NSLAB - 1, xev_cur, NSLAB * NCHS - 1)
    _ckpt("p2end")
    P2_T, P2_V, P2_S, P2_G = tT.n, tV.n, tS.n, tG.n

    E.op("scalar", lambda E_: _table_load(E_, set_gelu, nc))
    E.wait("sync", tT, P2_T)
    E.wait("sync", tV, P2_V)
    E.wait("sync", tS, P2_S)
    E.wait("sync", tG, P2_G)
    E.wait("sync", dTw, TERN_W)
    E.wait("sync", dTw2, TERN_W2)
    E.wait("sync", dTlp[0], dTlp[0].n)
    E.wait("sync", dTlp[1], dTlp[1].n)
    E.wait("sync", dXp[0], dXp[0].n)
    E.wait("sync", dXp[1], dXp[1].n)
    E.wait("sync", dQp[0], dQp[0].n)
    E.wait("sync", dQp[1], dQp[1].n)
    E.wait("sync", dR, dR.n)
    E.wait("sync", dX2, 16 * NSLAB)
    E.wait("sync", dX, dX.n)
    E.wait("sync", dQ, dQ.n)
    E.wait("sync", dW, dW.n)
    E.op("sync", lambda E_: E_.dma_start(
        out=w1t_sb[:, :, :], in_=w1ts_d[:, :, :].rearrange("e p m -> p e m")), inc=dC)
    E.op("sync", lambda E_: E_.dma_start(
        out=w2t_sb[:, :, :], in_=w2ts_d[:, :, :].rearrange("e p m -> p e m")), inc=dC)
    WT_LOADED = dC.n
    E.wait("tensor", tG, P2_G)
    E.wait("tensor", tV, P2_V)
    E.wait("tensor", tS, P2_S)
    E.wait("vector", tG, P2_G)
    E.wait("vector", dX2, 16 * NSLAB)
    E.wait("vector", dR, dR.n)
    E.wait("vector", dX, dX.n)
    E.wait("vector", dQ, dQ.n)
    E.wait("vector", dTlp[0], dTlp[0].n)
    E.wait("vector", dTlp[1], dTlp[1].n)
    E.wait("vector", dQp[0], dQp[0].n)
    E.wait("vector", dQp[1], dQp[1].n)
    E.wait("vector", dXp[0], dXp[0].n)
    E.wait("vector", dXp[1], dXp[1].n)
    E.wait("scalar", tG, P2_G)
    E.wait("scalar", dX2, 16 * NSLAB)
    E.wait("scalar", dR, dR.n)
    E.wait("scalar", dX, dX.n)
    E.wait("scalar", dQ, dQ.n)
    E.wait("scalar", dTlp[0], dTlp[0].n)
    E.wait("scalar", dTlp[1], dTlp[1].n)

    NCPC = CSLAB // 128
    prev_evac = {}
    xld = {}
    stage_done = {}

    def p3_load(t):
        sA = (t * CSLAB) // SLAB
        hh = t % (SLAB // CSLAB)
        buf = t % 2
        if t >= 2:
            E.wait("sync", tV, prev_evac[t - 2])
        E.op("sync", lambda E_, sA=sA, hh=hh, buf=buf: E_.dma_start(
            out=x2c[buf][:, :, :], in_=x2s_d[sA, :, :, CSLAB * hh:CSLAB * (hh + 1)]),
            inc=dCx[buf])
        xld[t] = dCx[buf].n

    def p3_stage(t):
        sA = (t * CSLAB) // SLAB
        hh = t % (SLAB // CSLAB)
        buf = t % 2
        vc8 = E.op("vector", lambda E_, sA=sA: E_.tensor_copy(
            out=st8b[:, :], in_=stats2_sb[:, sA, :]), inc=tV)
        E.wait("tensor", tV, vc8)
        E.wait("tensor", tV, S["b6_free"])
        ttr = E.op("tensor", lambda E_: E_.transpose(
            PB(6)[:, :].bitcast(BF16)[0:8, 0:128], st8b[:, :], ident_b[:, :]), inc=tT)
        E.wait("vector", tT, ttr)
        vrw = E.op("vector", lambda E_: E_.tensor_copy(
            out=rows2_sb[:, :], in_=PB(6)[:, :].bitcast(BF16)[0:8, 0:128]), inc=tV)
        S["b6_free"] = vrw
        E.wait("tensor", tV, vrw)
        E.wait("tensor", tV, S["b7V"])
        for g in range(NCPC):
            gg = hh * NCPC + g
            E.op("tensor", lambda E_, g=g, gg=gg: E_.matmul(
                PB(7)[:, 128 * g:128 * (g + 1)], pick_sb[:, gg, :], rows2_sb[:, :],
                start=True, stop=True))
            last = E.op("tensor", lambda E_, g=g, gg=gg: E_.matmul(
                PB(7)[:, 256 + 128 * g:256 + 128 * (g + 1)], pick_sb[:, 4 + gg, :],
                rows2_sb[:, :], start=True, stop=True),
                inc=(tT if g == NCPC - 1 else None))
        E.wait("vector", tT, last)
        E.op("vector", lambda E_, buf=buf: E_.tensor_copy(
            out=repmr_c[buf][:, :], in_=PB(7)[:, 0:CSLAB]))
        vrep = E.op("vector", lambda E_, buf=buf: E_.tensor_copy(
            out=repr_c[buf][:, :], in_=PB(7)[:, 256:256 + CSLAB]), inc=tV)
        S["b7V"] = vrep
        E.wait("vector", tV, vrep)
        E.wait("vector", dCx[buf], xld[t])
        for ec in range(EC):
            E.op("vector", lambda E_, ec=ec, buf=buf: E_.tensor_tensor(
                out=tmp_c[:, :], in0=x2c[buf][:, ec, :], in1=repr_c[buf][:, :], op=ALU.mult))
            last = E.op("vector", lambda E_, ec=ec, buf=buf: E_.tensor_tensor(
                out=x2ln_c[buf][:, ec, :], in0=tmp_c[:, :], in1=repmr_c[buf][:, :],
                op=ALU.subtract), inc=(tV if ec == EC - 1 else None))
        stage_done[t] = last

    p3_load(0)
    p3_stage(0)
    _ckpt("p3s0")
    sh = {}
    t2v = {}
    for t in range(NCS):
        buf = t % 2
        E.wait("tensor", tV, stage_done[t])
        E.wait("tensor", dC, WT_LOADED)
        if t + 1 < NCS:
            p3_load(t + 1)
        def ffn2(fc):
            hb = fc % 2
            E.wait("tensor", tS, sh[fc])
            for ec in range(EC):
                last = E.op("tensor", lambda E_, fc=fc, ec=ec, hb=hb: E_.matmul(
                    PB(ec // 2)[:, 256 * (ec % 2):256 * (ec % 2) + CSLAB],
                    w2t_sb[:, fc, 128 * ec:128 * (ec + 1)], h_sb[hb][:, :],
                    start=(fc == 0 and ec % 2 == 0),
                    stop=(fc == NFC - 1 and ec % 2 == 1)),
                    inc=(tT if ec == EC - 1 else None))
            t2v[fc] = last

        for fc in range(NFC):
            hb = fc % 2
            if fc >= 2:
                E.wait("tensor", tS, sh[fc - 2])
            for kc in range(EC):
                last = E.op("tensor", lambda E_, fc=fc, kc=kc, hb=hb, buf=buf: E_.matmul(
                    PB(4 + hb)[:, 0:CSLAB], w1t_sb[:, kc, 128 * fc:128 * (fc + 1)],
                    x2ln_c[buf][:, kc, :], start=(kc == 0), stop=(kc == EC - 1)),
                    inc=(tT if kc == EC - 1 else None))
            t1v = last
            E.wait("scalar", tT, t1v)
            if fc >= 2:
                E.wait("scalar", tT, t2v[fc - 2])
            sh[fc] = E.op("scalar", lambda E_, fc=fc, hb=hb: E_.activation(
                out=h_sb[hb][:, :], in_=PB(4 + hb)[:, 0:CSLAB], func=ACT.Gelu,
                bias=b1_sb[:, fc:fc + 1]), inc=tS)
            if fc >= 1:
                ffn2(fc - 1)
            if fc == 4 and t + 1 < NCS:
                p3_stage(t + 1)
        ffn2(NFC - 1)
        E.wait("vector", tT, t2v[NFC - 1])
        if t >= 2:
            E.wait("vector", dY, 16 * (t - 1))
        for ec in range(EC):
            last = E.op("vector", lambda E_, ec=ec, buf=buf: E_.scalar_tensor_tensor(
                out=outc[buf][:, ec, :],
                in0=PB(ec // 2)[:, 256 * (ec % 2):256 * (ec % 2) + CSLAB],
                scalar=b2_sb[:, ec:ec + 1], in1=x2c[buf][:, ec, :],
                op0=ALU.add, op1=ALU.add), inc=(tV if ec == EC - 1 else None))
        prev_evac[t] = last
        E.wait("tensor", tV, prev_evac[t])
        E.wait("sync", tV, prev_evac[t])
        E.op("sync", lambda E_, t=t, buf=buf: E_.dma_start(
            out=y_d[t, :, :, :], in_=outc[buf][:, :, :]), inc=dY)
    E.wait("sync", dY, dY.n)

    if dump:
        E.frozen = False
        _dvars = dict(
            e_sb=e_sb, p_sb=p_sb, pt_sb=pt_sb, o_sl=o_sl[0],
            **{f'o_c{i}': o_sl[0][:, :, 64*i:64*(i+1)] for i in range(8)}, sums_sb=sums_sb,
            rcp_sb=rcp_sb, gz_sb=gz_sb, pa_sb=pa_sb, u_sb=u_sb,
            s_sb=s_sb, maps_sb=maps_sb, st_sb=st_sb,
            win_f0=win_f[0], win_f1=win_f[1], wo1_f0=wo1_f[0], wo1_f1=wo1_f[1],
            wo2_f0=wo2_f[0], wo2_f1=wo2_f[1], win_b=win_b, wo1_b=wo1_b,
            wo2_b=wo2_b, ktr0=ktr[0], ktr1=ktr[1], vtr0=vtr[0], vtr1=vtr[1],
            x2_sl=x2_sl, stats2_sb=stats2_sb, stats1_sb=stats1_sb,
            thr_col=thr_col, rb_sb=rb_sb, mrb_sb=mrb_sb, rows_sb=rows_sb,
            scat_sb=scat_sb, wtpart=wtpart, lrwdc_all=lrwdc_all,
            kt_sl=kt_sl, vt_sl=vt_sl,
            x2c0=x2c[0], x2ln0=x2ln_c[0], reprc0=repr_c[0], repmrc0=repmr_c[0],
            x2c1=x2c[1], x2ln1=x2ln_c[1], reprc1=repr_c[1], repmrc1=repmr_c[1],
            rows2_sb=rows2_sb,
        )
        dS = sem("dS", 16)
        E.wait("sync", tT, tT.n)
        E.wait("sync", tV, tV.n)
        E.wait("sync", tS, tS.n)
        E.wait("sync", tG, tG.n)
        for name in dump:
            h = _dvars[name]
            shp = list(h.shape)
            dt = h.dtype
            dd = dt_("dump_" + name, shp, dt, kind="ExternalOutput").ap()
            idx = tuple([slice(None)] * len(shp))
            E.op("sync", lambda E_, d=dd, h=h, ix=idx: E_.dma_start(out=d[ix], in_=h[ix]), inc=dS)
        E.wait("sync", dS, dS.n)

    with nc.Block() as block:
        E.replay(block)
    return nc




_CACHE = {}
_LAYOUT = {}


def _host_prep(inputs):
    f32 = np.float32
    Wq = np.asarray(inputs["L0_attn_Wq"], f32)
    Wk = np.asarray(inputs["L0_attn_Wk"], f32)
    Wv = np.asarray(inputs["L0_attn_Wv"], f32)
    Wlr = np.asarray(inputs["L0_attn_Wlr"], f32)
    Wbeta = np.asarray(inputs["L0_attn_Wbeta"], f32)
    Wo = np.asarray(inputs["L0_attn_Wo"], f32)
    Wini = np.asarray(inputs["L0_attn_W_in_init"], f32)[0]
    Wouti = np.asarray(inputs["L0_attn_W_out_init"], f32)[0]
    W1 = np.asarray(inputs["L0_ffn_W1"], f32)
    b1 = np.asarray(inputs["L0_ffn_b1"], f32)
    W2 = np.asarray(inputs["L0_ffn_W2"], f32)
    b2 = np.asarray(inputs["L0_ffn_b2"], f32)

    def lhsT(w, out_w):
        return np.ascontiguousarray(w.T.reshape(w.shape[1] // 128, 128, out_w))

    d = {}
    d["wq"] = lhsT(Wq, DM)
    d["wk"] = lhsT(Wk, DM)
    d["wv"] = lhsT(Wv, DM)
    d["wlb"] = lhsT(np.concatenate([Wlr, Wbeta], axis=0), 64)
    d["wo"] = lhsT(Wo, DM)
    d["w1"] = lhsT(W1, FF)
    d["w2"] = np.ascontiguousarray(W2.T.reshape(NFC, 128, DM))
    d["b1a"] = np.ascontiguousarray(b1.reshape(NFC, 128).T)
    d["b2a"] = np.ascontiguousarray(b2.reshape(EC, 128).T)
    wini = np.zeros((128, NP * DH), f32)
    wo1i = np.zeros((128, NP * DH), f32)
    wo2i = np.zeros((128, NP * DH), f32)
    for cpar in range(2):
        for r in range(DH):
            part = cpar * 64 + r
            for p in range(NP):
                h = 2 * p + cpar
                wini[part, p * DH:(p + 1) * DH] = Wini[:, h, r]
                wo1i[part, p * DH:(p + 1) * DH] = Wouti[r, h, :]
                wo2i[part, p * DH:(p + 1) * DH] = Wouti[:, h, r]
    d["wini"] = wini
    d["wo1i"] = wo1i
    d["wo2i"] = wo2i
    d["idn"] = np.eye(128, dtype=f32)
    sel = np.zeros((64, 4, 128), f32)
    for m, base in enumerate((0, 16, 32, 48)):
        for v in range(base, base + 16):
            head = v - base
            q0 = 64 * (head % 2)
            sel[v, m, q0:q0 + 64] = 1.0
    d["selm"] = np.ascontiguousarray(sel.reshape(64, 4 * 128))
    maskp = np.zeros((64, NP), f32)
    for v in range(64):
        maskp[v, (v % 16) // 2] = 1.0
    d["maskp"] = maskp
    d["ones1"] = np.ones((1, 128), f32)
    d["onesc"] = np.ones((128, 1), f32)
    d["cst"] = np.tile(np.array([[EPS, -0.5, 0.5, 0.0, 1.0, 0, 0, 0]], f32), (128, 1))
    pick = np.zeros((8, 8, 128), f32)
    for g in range(8):
        pick[g, g, :] = 1.0
    d["pick"] = np.ascontiguousarray(pick.reshape(8, 8 * 128))
    return d


def kernel(**inputs):
    L = 4096
    if "nc" not in _CACHE:
        _CACHE["nc"] = build_nc(L)
    nc = _CACHE["nc"]
    shared = _host_prep(inputs)
    x = np.asarray(inputs["L0_attn_x"], np.float32)
    B = x.shape[0]
    NSLAB = L // SLAB
    in_maps = []
    for b in range(B):
        xt = np.ascontiguousarray(x[b].T)
        xa = xt.reshape(EC, 128, NSLAB, SLAB).transpose(2, 1, 0, 3)
        m = dict(shared)
        m["x"] = np.ascontiguousarray(xa)
        in_maps.append(m)
    res = run_bass_kernel_spmd(nc, in_maps, core_ids=list(range(B)))
    outs = []
    for b in range(B):
        y = res.results[b]["y"]
        yt = np.transpose(y, (2, 1, 0, 3)).reshape(DM, L)
        outs.append(yt.T)
    return np.ascontiguousarray(np.stack(outs), dtype=np.float32)
